# version 11
# baseline (speedup 1.0000x reference)
import math
import numpy as np
import ml_dtypes
import concourse.bass as bass
import concourse.mybir as mybir
from concourse.bass_utils import run_bass_kernel_spmd

F32 = mybir.dt.float32
BF16 = mybir.dt.bfloat16
AF = mybir.ActivationFunctionType
ALU = mybir.AluOpType
AX = mybir.AxisListType

NEG = -1.0e30
ENGS = ("pe", "act", "dve", "pool", "sp")


class Cfg:
    def __init__(self, S=2048, D=2048, QL=1536, KL=512, H=16, F=1024, E=16, SH=2,
                 WL=512, debug=False):
        self.S, self.D, self.QL, self.KL, self.H, self.F, self.E, self.SH = \
            S, D, QL, KL, H, F, E, SH
        self.CD = D // 128
        self.QC = QL // 128
        self.KC = KL // 128
        self.FC = F // 128
        self.NE = E + SH
        self.TT = S // 2
        self.TH = S // 2
        self.WL = WL
        self.NSTRIP = S // WL
        self.MT = S // 128
        self.NT = S // 128
        self.scale = 1.0 / math.sqrt(192.0)
        self.eps_rms = 1e-6
        self.eps_ln = 1e-5
        self.debug = debug
        self.stop_after = None
        self.safe_barriers = None


class Prog:
    def __init__(self, nc, cfg):
        self.nc = nc
        self.cfg = cfg
        self.q = {e: [] for e in ENGS}
        self.sems = {}
        self.cnt = {}
        self.dumps = []

    def sem(self, name):
        self.sems[name] = self.nc.alloc_semaphore(name)
        self.cnt[name] = 0

    def wait(self, eng, name, val):
        if val is None or val <= 0:
            return
        h = self.sems[name]
        self.q[eng].append(lambda e, h=h, v=val: e.wait_ge(h, v))

    def op(self, eng, fn, sem=None, inc=1):
        if sem is None:
            self.q[eng].append(fn)
            return None
        self.cnt[sem] += inc
        h = self.sems[sem]
        self.q[eng].append(lambda e, fn=fn, h=h, inc=inc: fn(e).then_inc(h, inc))
        return self.cnt[sem]

    def barrier(self):
        vals = dict(self.cnt)
        for eng in ENGS:
            for name, v in vals.items():
                self.wait(eng, name, v)

    def dump(self, name, ap, shape, dt=F32):
        if not self.cfg.debug:
            return
        d = self.nc.dram_tensor("dbg_" + name, list(shape), dt,
                                kind="ExternalOutput").ap()
        self.dumps.append(name)
        for s in ("pe", "act", "dve", "pool"):
            self.wait("sp", s, self.cnt[s])
        self.wait("sp", "so", self.cnt["so"])
        self.op("sp", lambda e, d=d, a=ap: e.dma_start(d, a), sem="so", inc=16)
        self.barrier()

    def finish(self):
        with self.nc.Block() as blk:
            @blk.tensor
            def _(t):
                for f in self.q["pe"]:
                    f(t)

            @blk.scalar
            def _(s):
                for f in self.q["act"]:
                    f(s)

            @blk.vector
            def _(v):
                for f in self.q["dve"]:
                    f(v)

            @blk.gpsimd
            def _(g):
                for f in self.q["pool"]:
                    f(g)

            @blk.sync
            def _(s):
                for f in self.q["sp"]:
                    f(s)


def tile512(width):
    o = 0
    while o < width:
        w = min(512, width - o)
        yield o, w
        o += w


def build(cfg: Cfg):
    nc = bass.Bass("TRN2", target_bir_lowering=False)
    S, D, CD, QC, KC, H, FC, E, SH, NE = (cfg.S, cfg.D, cfg.CD, cfg.QC, cfg.KC,
                                          cfg.H, cfg.FC, cfg.E, cfg.SH, cfg.NE)
    QL, KL, F, TH, WL, MT, NT, TT = (cfg.QL, cfg.KL, cfg.F, cfg.TH, cfg.WL,
                                     cfg.MT, cfg.NT, cfg.TT)
    F2, D2 = F // 2, D // 2
    assert CD * F2 == FC * D2

    def din(name, shape):
        return nc.dram_tensor(name, list(shape), F32, kind="ExternalInput").ap()

    x_fm = din("x_fm", [D, S])
    wqa_re = din("wqa_re", [4, 128, CD * (QL // 4)])
    gq_re = din("gq_re", [128, QC])
    wqb_n_re = din("wqb_n_re", [H, 128, QC * 128])
    wqb_r2_re = din("wqb_r2_re", [H // 2, 128, QC * 128])
    wkva_re = din("wkva_re", [128, CD * (KL + 64)])
    gkv_re = din("gkv_re", [128, KC])
    wkvb_k_re = din("wkvb_k_re", [H, 128, KC * 128])
    wkvb_v_re = din("wkvb_v_re", [H, 128, KC * 128])
    wo_re = din("wo_re", [CD, 128, H * 128])
    wa_rr = din("wa_rr", [128, CD * E])
    gb_re = din("gb_re", [128, NE * FC])
    b1_re = din("b1_re", [128, NE * FC])
    b2r_re = din("b2r_re", [E, D])
    sb2_cols = din("sb2_cols", [128, SH * CD])
    rg_re = din("rg_re", [E, 2, 128, CD * F2])
    rw1_re = din("rw1_re", [E, 2, 128, CD * F2])
    rw2_re = din("rw2_re", [E, 2, 128, FC * D2])
    sg_re = din("sg_re", [SH, 2, 128, CD * F2])
    sw1_re = din("sw1_re", [SH, 2, 128, CD * F2])
    sw2_re = din("sw2_re", [SH, 2, 128, FC * D2])
    esel_re = din("esel_re", [E, E * 128])
    cos_re = din("cos_re", [64, S])
    sin_re = din("sin_re", [64, S])
    maskT_re = din("maskT_re", [128, 128])
    ident_re = din("ident_re", [128, 128])

    x1pre_hbm = nc.dram_tensor("x1pre_hbm", [D, S], F32, kind="Internal").ap()
    x1_hbm = nc.dram_tensor("x1_hbm", [D, S], F32, kind="Internal").ap()
    oh_hbm = nc.dram_tensor("oh_hbm", [H, 128, S], BF16, kind="Internal").ap()
    out_fm = nc.dram_tensor("out_fm", [D, S], F32, kind="ExternalOutput").ap()

    P = Prog(nc, cfg)
    for s in ("cst", "xbf", "wq0", "wq1", "wh0", "wh1", "wr0", "wr1",
              "sx", "sx0", "sx1", "so", "os0", "os1", "pe", "act", "dve",
              "pool"):
        P.sem(s)

    def sb(name, shape, dt):
        return nc.alloc_sbuf_tensor(name, list(shape), dt)

    def psum(name, shape):
        return nc.alloc_psum_tensor(name, list(shape), F32)

    cos_t = sb("cos_t", [64, S], BF16)
    sin_t = sb("sin_t", [64, S], BF16)
    maskT = sb("maskT", [128, 128], F32)
    ident = sb("ident", [128, 128], F32)
    gq = sb("gq", [128, QC], F32)
    gkv = sb("gkv", [128, KC], F32)
    wa_sb = sb("wa_sb", [128, CD * E], F32)
    gb_sb = sb("gb_sb", [128, NE * FC], F32)
    b1_sb = sb("b1_sb", [128, NE * FC], F32)
    sb2c = sb("sb2c", [128, SH * CD], F32)
    ssb2 = sb("ssb2", [128, CD], F32)
    c_bf = sb("c_bf", [E, S], BF16)
    ones_col_bf = sb("ones_col_bf", [128, 1], BF16)
    ones_col_f = sb("ones_col_f", [128, 1], F32)
    ones_row_bf = sb("ones_row_bf", [1, 128], BF16)
    ones_row_f = sb("ones_row_f", [1, 128], F32)
    kmax2 = sb("kmax2", [1, 4], F32)
    zrow = sb("zrow", [1, max(E, 16)], F32)
    fconst = sb("fconst", [1, 8], F32)
    sc192 = sb("sc192", [128, 1], F32)
    sb_pers = nc.sbuf_base

    P.op("pool", lambda g: g.dma_start(cos_t[:, :], cos_re), sem="cst", inc=16)
    P.op("pool", lambda g: g.dma_start(sin_t[:, :], sin_re), sem="cst", inc=16)
    for dst, src in ((maskT, maskT_re), (ident, ident_re), (gq, gq_re),
                     (gkv, gkv_re), (wa_sb, wa_rr), (gb_sb, gb_re),
                     (b1_sb, b1_re), (sb2c, sb2_cols)):
        P.op("sp", lambda s, d=dst, r=src: s.dma_start(d[:, :], r), sem="sx",
             inc=16)
    sx_consts = P.cnt["sx"]
    P.op("pool", lambda g: g.memset(ones_col_bf[:, :], 1.0), sem="pool")
    P.op("pool", lambda g: g.memset(ones_col_f[:, :], 1.0), sem="pool")
    P.op("pool", lambda g: g.memset(ones_row_bf[:, :], 1.0), sem="pool")
    P.op("pool", lambda g: g.memset(ones_row_f[:, :], 1.0), sem="pool")
    P.op("pool", lambda g: g.memset(zrow[:, :], 0.0), sem="pool")
    P.op("pool", lambda g: g.memset(fconst[0:1, 0:1], cfg.eps_rms), sem="pool")
    P.op("pool", lambda g: g.memset(fconst[0:1, 1:2], 1.0 / KL), sem="pool")
    P.op("pool", lambda g: g.memset(fconst[0:1, 2:3], 1.0 / QL), sem="pool")
    P.op("pool", lambda g: g.memset(fconst[0:1, 3:4], cfg.eps_ln), sem="pool")
    P.op("pool", lambda g: g.memset(sc192[:, :], cfg.scale), sem="pool")
    pool_init = P.cnt["pool"]
    P.wait("dve", "sx", sx_consts)
    dve_ssb2 = P.op("dve", lambda v: v.tensor_add(
        ssb2[:, :], sb2c[:, 0:CD], sb2c[:, CD:2 * CD]), sem="dve")

    kvn = sb("kvn", [128, KC * S], BF16)
    kra = sb("kra", [128, S], BF16)
    qn = sb("qn", [128, QC * S], BF16)
    sb_markA = nc.sbuf_base

    x_bf = sb("x_bf", [128, CD * S], BF16)
    rstd_row = sb("rstd_row", [1, S], F32)
    sq_a = sb("sq_a", [128, S], BF16)
    rope_tmp = sb("rope_tmp", [128, S], BF16)
    kr_f = sb("kr_f", [64, S], BF16)
    sb_markW = nc.sbuf_base
    wkva_bf = sb("wkva_bf", [128, CD * (KL + 64)], BF16)
    assert nc.sbuf_base <= nc.sbuf_top, f"A1 sbuf over {nc.sbuf_base}"
    nc.sbuf_base = sb_markW
    wqa_bf = [sb("wqa_bf0", [128, CD * (QL // 4)], BF16),
              sb("wqa_bf1", [128, CD * (QL // 4)], BF16)]
    assert nc.sbuf_base <= nc.sbuf_top, f"A2 sbuf over {nc.sbuf_base}"

    P.op("pool", lambda g: g.dma_start(wkva_bf[:, :], wkva_re), sem="cst", inc=16)
    cst_wkva = P.cnt["cst"]
    P.op("pool", lambda g: g.memset(kra[64:65, :], -1.0), sem="pool")
    pool_kra = P.cnt["pool"]
    for dc in range(CD):
        P.op("pool", lambda g, dc=dc: g.dma_start(
            x_bf[:, dc * S:(dc + 1) * S], x_fm[dc * 128:(dc + 1) * 128, :]),
            sem="xbf", inc=16)
    xbf_x = P.cnt["xbf"]

    def early_out():
        P.wait("sp", "so", P.cnt["so"])
        P.wait("sp", "os0", P.cnt["os0"])
        P.wait("sp", "os1", P.cnt["os1"])
        P.finish()

    if cfg.stop_after == "P0":
        early_out()
        return nc, P

    pv = [psum("pv0", [128, TH]), psum("pv1", [128, TH])]
    ps_shared_mark = nc.psum_base
    pst = psum("pst", [1, S])
    nc.psum_base = ps_shared_mark
    pbc = psum("pbc", [128, S])
    assert nc.psum_base <= 8, f"A12 psum over {nc.psum_base}"

    def proj(nchunks, lhs_of, dst_of, chunk_waits, denom, first_waits,
             do_stats=True):
        pv_free = {0: (P.cnt["act"], P.cnt["dve"]),
                   1: (P.cnt["act"], P.cnt["dve"])}
        pend_stats = []
        for c in range(nchunks):
            lhs_fn, cw = lhs_of(c)
            for nm, v in chunk_waits(c):
                P.wait("pe", nm, v)
            for th in range(2):
                bi = (c * 2 + th) % 2
                if bi in pv_free:
                    P.wait("pe", "act", pv_free[bi][0])
                    P.wait("pe", "dve", pv_free[bi][1])
                if c == 0 and th == 0:
                    for nm, v in first_waits:
                        P.wait("pe", nm, v)
                for dc in range(CD):
                    for o, w in tile512(TH):
                        sem = "pe" if (dc == CD - 1 and o + w == TH) else None
                        P.op("pe", lambda t, a=pv[bi][0:cw, o:o + w],
                             b=lhs_fn(dc),
                             r=x_bf[:, dc * S + th * TH + o:
                                    dc * S + th * TH + o + w],
                             s1=(dc == 0), s2=(dc == CD - 1):
                             t.matmul(a, b, r, start=s1, stop=s2), sem=sem)
                pe_mm = P.cnt["pe"]
                P.wait("act", "pe", pe_mm)
                a_sq = P.op("act", lambda s, p=pv[bi][0:cw, 0:TH],
                            d=sq_a[0:cw, th * TH:(th + 1) * TH]:
                            s.activation(d, p, AF.Square), sem="act")
                P.wait("dve", "pe", pe_mm)
                P.wait("dve", "act", a_sq)
                d_cp = P.op("dve", lambda v, p=pv[bi][0:cw, 0:TH],
                            d=dst_of(c, th): v.tensor_copy(d, p), sem="dve")
                if not do_stats:
                    pv_free[bi] = (a_sq, d_cp)
                    continue
                def emit_stats(c=c, th=th, cw=cw, a_sq=a_sq):
                    P.wait("pe", "act", a_sq)
                    for o, w in tile512(TH):
                        first_in_reg = ((th * TH + o) * 4) % 2048 == 0
                        last_in_reg = ((th * TH + o + w) * 4) % 2048 == 0 or \
                            (th == 1 and o + w == TH)
                        sem = "pe" if o + w == TH else None
                        P.op("pe", lambda t, o=o, w=w, cw=cw, th=th, c=c,
                             s1=(c == 0 and first_in_reg),
                             s2=(c == nchunks - 1 and last_in_reg):
                             t.matmul(pst[0:1, th * TH + o:th * TH + o + w],
                                      ones_col_bf[0:cw, :],
                                      sq_a[0:cw, th * TH + o:th * TH + o + w],
                                      start=s1, stop=s2, skip_group_check=True),
                             sem=sem)
                pend_stats.append(emit_stats)
                if len(pend_stats) > 1:
                    pend_stats.pop(0)()
                pv_free[bi] = (a_sq, d_cp)
        while pend_stats:
            pend_stats.pop(0)()
        pe_stats = P.cnt["pe"]
        if not do_stats:
            return 0, 0, pv_free
        P.wait("act", "pe", pe_stats)
        a1 = P.op("act", lambda s, dn=denom: s.activation(
            rstd_row[:, :], pst[0:1, :], AF.Sqrt, bias=fconst[0:1, 0:1],
            scale=fconst[0:1, 1:2] if dn == KL else fconst[0:1, 2:3]),
            sem="act")
        P.wait("dve", "act", a1)
        d2 = P.op("dve", lambda v: v.reciprocal(rstd_row[:, :], rstd_row[:, :]),
                  sem="dve")
        return a1, d2, pv_free

    def bcast_row_f32(row_ap, dst_psum, width, extra_pe_waits):
        for nm, v in extra_pe_waits:
            P.wait("pe", nm, v)
        last = None
        for o, w in tile512(width):
            sem = "pe" if o + w == width else None
            last = P.op("pe", lambda t, o=o, w=w, r=row_ap, d=dst_psum: t.matmul(
                d[:, o:o + w], ones_row_f[:, :], r[0:1, o:o + w],
                start=True, stop=True), sem=sem)
        return P.cnt["pe"]

    first_waits = [("cst", cst_wkva), ("xbf", xbf_x),
                   ("pool", pool_init)]

    def kva_lhs(c):
        return (lambda dc, c=c: wkva_bf[:, dc * (KL + 64) + c * 128:
                                        dc * (KL + 64) + c * 128 + 128], 128)

    stats_evac, dve_kv_rstd, pvfree1 = proj(
        KC, kva_lhs,
        lambda c, th: kvn[:, c * S + th * TH:c * S + th * TH + TH],
        lambda c: [], KL, first_waits,
        do_stats=(cfg.stop_after != "A1c1"))

    if cfg.stop_after in ("A1c1", "A1c2"):
        early_out()
        return nc, P

    act_b4 = P.cnt["act"]
    dve_b4 = P.cnt["dve"]
    for th in range(2):
        bi = th % 2
        P.wait("pe", "act", act_b4)
        P.wait("pe", "dve", dve_b4)
        for dc in range(CD):
            for o, w in tile512(TH):
                sem = "pe" if (dc == CD - 1 and o + w == TH) else None
                P.op("pe", lambda t, bi=bi, dc=dc, th=th, o=o, w=w: t.matmul(
                    pv[bi][0:64, o:o + w],
                    wkva_bf[:, dc * (KL + 64) + KL:dc * (KL + 64) + KL + 64],
                    x_bf[:, dc * S + th * TH + o:dc * S + th * TH + o + w],
                    start=(dc == 0), stop=(dc == CD - 1)), sem=sem)
        pe_mm = P.cnt["pe"]
        P.wait("act", "pe", pe_mm)
        P.op("act", lambda s, bi=bi, th=th: s.activation(
            kr_f[0:64, th * TH:(th + 1) * TH], pv[bi][0:64, 0:TH], AF.Copy),
            sem="act")
    act_krf = P.cnt["act"]

    if cfg.stop_after == "A1c3":
        early_out()
        return nc, P

    pe_bc1 = bcast_row_f32(rstd_row, pbc, S, [("dve", dve_kv_rstd)])
    P.wait("dve", "pe", pe_bc1)
    for c in range(KC):
        d1 = P.op("dve", lambda v, c=c: v.tensor_mul(
            kvn[:, c * S:(c + 1) * S], kvn[:, c * S:(c + 1) * S], pbc[:, :]),
            sem="dve")
        P.wait("dve", "dve", d1)
        P.op("dve", lambda v, c=c: v.tensor_scalar_mul(
            kvn[:, c * S:(c + 1) * S], kvn[:, c * S:(c + 1) * S],
            gkv[:, c:c + 1]), sem="dve")
    dve_kvn = P.cnt["dve"]

    if cfg.stop_after == "A1c4":
        early_out()
        return nc, P

    P.wait("dve", "act", act_krf)
    P.wait("dve", "cst", P.cnt["cst"])
    e1 = P.op("dve", lambda v: v.tensor_copy(rope_tmp[0:32, :], kr_f[32:64, :]),
              sem="dve")
    e2 = P.op("dve", lambda v: v.tensor_copy(rope_tmp[32:64, :], kr_f[0:32, :]),
              sem="dve")
    P.wait("dve", "dve", e2)
    e3 = P.op("dve", lambda v: v.tensor_mul(rope_tmp[0:64, :], rope_tmp[0:64, :],
                                            sin_t[:, :]), sem="dve")
    e4 = P.op("dve", lambda v: v.tensor_mul(kr_f[0:64, :], kr_f[0:64, :],
                                            cos_t[:, :]), sem="dve")
    P.wait("dve", "dve", e4)
    P.wait("dve", "pool", pool_kra)
    P.op("dve", lambda v: v.tensor_add(kra[0:64, :], rope_tmp[0:64, :],
                                       kr_f[0:64, :]), sem="dve")
    dve_kra = P.cnt["dve"]
    P.dump("kvn", kvn[:, :], [128, KC * S], BF16)
    P.dump("kra", kra[0:65, :], [65, S], BF16)

    if cfg.stop_after == "A1":
        early_out()
        return nc, P

    QQ = QC // 4
    wq_cnt = [0, 0]
    pe_a1_end = P.cnt["pe"]
    quarter_end = {}

    def qa_chunk_waits(c):
        qh, j = divmod(c, QQ)
        sl = qh % 2
        if j == 0:
            P.wait("pool", "pe", quarter_end.get(qh - 2, pe_a1_end))
            wq_cnt[sl] += 1
            P.op("pool", lambda g, qh=qh, sl=sl: g.dma_start(
                wqa_bf[sl][:, :], wqa_re[qh]), sem="wq%d" % sl, inc=16)
        if c > 0 and j == 0:
            quarter_end[qh - 1] = P.cnt["pe"]
        return [("wq%d" % sl, wq_cnt[sl] * 16)]

    def qa_lhs(c):
        qh, j = divmod(c, QQ)
        sl = qh % 2
        return (lambda dc, j=j, sl=sl: wqa_bf[sl][:, dc * (QL // 4) + j * 128:
                                                  dc * (QL // 4) + j * 128 + 128],
                128)

    _, dve_q_rstd, pvfree2 = proj(
        QC, qa_lhs,
        lambda c, th: qn[:, c * S + th * TH:c * S + th * TH + TH],
        qa_chunk_waits, QL, [("dve", dve_kvn)])

    pe_bc2 = bcast_row_f32(rstd_row, pbc, S, [("dve", dve_q_rstd)])
    P.wait("dve", "pe", pe_bc2)
    for c in range(QC):
        d1 = P.op("dve", lambda v, c=c: v.tensor_mul(
            qn[:, c * S:(c + 1) * S], qn[:, c * S:(c + 1) * S], pbc[:, :]),
            sem="dve")
        P.wait("dve", "dve", d1)
        P.op("dve", lambda v, c=c: v.tensor_scalar_mul(
            qn[:, c * S:(c + 1) * S], qn[:, c * S:(c + 1) * S], gq[:, c:c + 1]),
            sem="dve")
    P.dump("qn", qn[:, :], [128, QC * S], BF16)

    if cfg.stop_after == "A2":
        early_out()
        return nc, P

    P.barrier()
    nc.sbuf_base = sb_markA
    nc.psum_base = 0

    q_nope = [sb("q_nope0", [128, S], BF16), sb("q_nope1", [128, S], BF16)]
    q_aug = [sb("q_aug0", [128, S], BF16), sb("q_aug1", [128, S], BF16)]
    k_nope = [sb("k_nope0", [128, S], BF16), sb("k_nope1", [128, S], BF16)]
    v_tm = [sb("v_tm0", [128, S], BF16), sb("v_tm1", [128, S], BF16)]
    wkvbk_sb = [sb("wkvbk0", [128, KC * 128], BF16),
                sb("wkvbk1", [128, KC * 128], BF16)]
    wkvbv_sb = [sb("wkvbv0", [128, KC * 128], BF16),
                sb("wkvbv1", [128, KC * 128], BF16)]
    wqbn_sb = [sb("wqbn0", [128, QC * 128], BF16),
               sb("wqbn1", [128, QC * 128], BF16)]
    wqbr_sb = [sb("wqbr0", [128, QC * 128], BF16),
               sb("wqbr1", [128, QC * 128], BF16)]
    qr2_bf = sb("qr2_bf", [128, S], BF16)
    rope_t3 = sb("rope_t3", [128, S], BF16)
    sq_h = sb("sq_h", [128, S], BF16)
    sq_kr = sb("sq_kr", [64, S], BF16)
    sq_qr = sb("sq_qr", [64, S], BF16)
    probsT = [sb("probsT0", [128, WL], BF16), sb("probsT1", [128, WL], BF16)]
    rs_row = sb("rs_row", [1, WL], F32)
    qss_row = sb("qss_row", [1, S], F32)
    rr_bf = sb("rr_bf", [1, WL], BF16)
    rb_sb = sb("rb_sb", [128, WL], F32)
    ostrip = [sb("ostrip0", [128, WL], BF16), sb("ostrip1", [128, WL], BF16)]
    assert nc.sbuf_base <= nc.sbuf_top, f"A3 sbuf over {nc.sbuf_base}"

    psb = [psum("psb0", [128, WL]), psum("psb1", [128, WL])]
    pav = psum("pav", [128, WL])
    prs = psum("prs", [1, WL])
    prb = psum("prb", [128, WL])
    pprep = [psum("pprep0", [128, 512]), psum("pprep1", [128, 512])]
    pkm = psum("pkm", [1, 512])
    assert nc.psum_base <= 8, f"A3 psum over {nc.psum_base}"

    P.wait("act", "dve", P.cnt["dve"])
    act_sqkr = P.op("act", lambda s: s.activation(
        sq_kr[0:64, :], kra[0:64, :], AF.Square), sem="act")

    whd_cnt = [0, 0]
    pkm_dve = [0]
    pkm_act_h = [0]
    qr2_read_dve = [0]
    head_prep_evac = {}
    head_main_done = {}
    prep_psum_free = {}
    probsT_free = {}
    scores_free = {}
    sqh_pe_free = [0]
    kmax_read = [0]
    st_avrs = {"pav": 0, "prs": 0, "prb": 0}
    pend_fin = [None]
    ostrip_free = {}
    ppi = [0]

    def head_loads(h):
        b = h % 2
        P.wait("pool", "pe", P.cnt["pe"])
        P.op("pool", lambda g, h=h, b=b: g.dma_start(
            wkvbk_sb[b][:, :], wkvb_k_re[h]), sem="wh%d" % b, inc=16)
        P.op("pool", lambda g, h=h, b=b: g.dma_start(
            wkvbv_sb[b][:, :], wkvb_v_re[h]), sem="wh%d" % b, inc=16)
        P.op("pool", lambda g, h=h, b=b: g.dma_start(
            wqbn_sb[b][:, :], wqb_n_re[h]), sem="wh%d" % b, inc=16)
        whd_cnt[b] += 3
        if h % 2 == 0:
            P.op("pool", lambda g, h=h, b=b: g.dma_start(
                wqbr_sb[b][:, :], wqb_r2_re[h // 2]), sem="wh%d" % b, inc=16)
            whd_cnt[b] += 1

    def next_pp():
        bi = ppi[0] % 2
        ppi[0] += 1
        fr = prep_psum_free.get(bi)
        if fr is not None:
            P.wait("pe", "act", fr[0])
            P.wait("pe", "dve", fr[1])
        return bi

    def prep(h):
        b = h % 2
        P.wait("pe", "wh%d" % b, whd_cnt[b] * 16)
        pc = head_main_done.get(h - 2)
        if pc is not None:
            P.wait("act", "pe", pc[0])
            P.wait("dve", "pe", pc[0])
        for mt in range(MT):
            bi = next_pp()
            for kc in range(KC):
                sem = "pe" if kc == KC - 1 else None
                P.op("pe", lambda t, bi=bi, kc=kc, mt=mt, b=b: t.matmul(
                    pprep[bi][:, 0:128],
                    kvn[:, kc * S + mt * 128:kc * S + mt * 128 + 128],
                    wkvbv_sb[b][:, kc * 128:(kc + 1) * 128],
                    start=(kc == 0), stop=(kc == KC - 1)), sem=sem)
            pe_mm = P.cnt["pe"]
            P.wait("act", "pe", pe_mm)
            a1 = P.op("act", lambda s, bi=bi, mt=mt, b=b: s.activation(
                v_tm[b][:, mt * 128:(mt + 1) * 128], pprep[bi][:, 0:128],
                AF.Copy), sem="act")
            prep_psum_free[bi] = (a1, P.cnt["dve"])
        if h % 2 == 0:
            for o, w in [(o, min(512, S - o)) for o in range(0, S, 512)]:
                bi = next_pp()
                for qc in range(QC):
                    sem = "pe" if qc == QC - 1 else None
                    P.op("pe", lambda t, bi=bi, qc=qc, o=o, w=w, b=b: t.matmul(
                        pprep[bi][:, 0:w],
                        wqbr_sb[b][:, qc * 128:(qc + 1) * 128],
                        qn[:, qc * S + o:qc * S + o + w],
                        start=(qc == 0), stop=(qc == QC - 1)), sem=sem)
                pe_mm = P.cnt["pe"]
                P.wait("act", "pe", pe_mm)
                P.wait("act", "dve", qr2_read_dve[0])
                a1 = P.op("act", lambda s, bi=bi, o=o, w=w: s.activation(
                    qr2_bf[:, o:o + w], pprep[bi][:, 0:w], AF.Copy), sem="act")
                prep_psum_free[bi] = (a1, P.cnt["dve"])
        rb0 = 0 if h % 2 == 0 else 64
        P.wait("dve", "act", P.cnt["act"])
        r1 = P.op("dve", lambda v, rb0=rb0: v.tensor_copy(
            rope_t3[0:32, :], qr2_bf[rb0 + 32:rb0 + 64, :]), sem="dve")
        r2 = P.op("dve", lambda v, rb0=rb0: v.tensor_copy(
            rope_t3[32:64, :], qr2_bf[rb0:rb0 + 32, :]), sem="dve")
        r2b = P.op("dve", lambda v, rb0=rb0, b=b: v.tensor_copy(
            q_aug[b][0:64, :], qr2_bf[rb0:rb0 + 64, :]), sem="dve")
        P.wait("dve", "dve", r2b)
        P.op("dve", lambda v: v.tensor_mul(rope_t3[0:64, :], rope_t3[0:64, :],
                                           sin_t[:, :]), sem="dve")
        r4 = P.op("dve", lambda v, b=b: v.tensor_mul(
            q_aug[b][0:64, :], q_aug[b][0:64, :], cos_t[:, :]), sem="dve")
        P.wait("dve", "dve", r4)
        r5 = P.op("dve", lambda v, b=b: v.tensor_add(
            q_aug[b][0:64, :], q_aug[b][0:64, :], rope_t3[0:64, :]), sem="dve")
        dve_rope = P.cnt["dve"]
        qr2_read_dve[0] = dve_rope

        ktiles = []
        for o, w in [(o, min(512, S - o)) for o in range(0, S, 512)]:
            bi = next_pp()
            for kc in range(KC):
                sem = "pe" if kc == KC - 1 else None
                P.op("pe", lambda t, bi=bi, kc=kc, o=o, w=w, b=b: t.matmul(
                    pprep[bi][:, 0:w],
                    wkvbk_sb[b][:, kc * 128:(kc + 1) * 128],
                    kvn[:, kc * S + o:kc * S + o + w],
                    start=(kc == 0), stop=(kc == KC - 1)), sem=sem)
            pe_mm = P.cnt["pe"]
            P.wait("act", "pe", pe_mm)
            a1 = P.op("act", lambda s, bi=bi, o=o, w=w, b=b: s.activation(
                k_nope[b][:, o:o + w], pprep[bi][:, 0:w], AF.Copy), sem="act")
            P.wait("act", "act", a1)
            P.wait("act", "pe", sqh_pe_free[0])
            a2 = P.op("act", lambda s, bi=bi, w=w, o=o: s.activation(
                sq_h[:, o:o + w], pprep[bi][:, 0:w], AF.Square), sem="act")
            prep_psum_free[bi] = (a2, P.cnt["dve"])
            ktiles.append((o, w, a2))
        first_o = True
        for o, w, a2 in ktiles:
            P.wait("pe", "act", a2)
            P.wait("pe", "act", act_sqkr)
            P.wait("pe", "act", pkm_act_h[0])
            P.wait("pe", "dve", pkm_dve[0])
            P.op("pe", lambda t, w=w, o=o: t.matmul(
                pkm[0:1, 0:w], ones_col_bf[:, :], sq_h[:, o:o + w],
                start=True, stop=False), sem=None)
            pm = P.op("pe", lambda t, w=w, o=o: t.matmul(
                pkm[0:1, 0:w], ones_col_bf[0:64, :], sq_kr[0:64, o:o + w],
                start=False, stop=True), sem="pe")
            sqh_pe_free[0] = pm
            P.wait("dve", "pe", pm)
            if first_o:
                d1 = P.op("dve", lambda v, w=w: v.reduce_max(
                    kmax2[0:1, 0:1], pkm[0:1, 0:w], axis=AX.X), sem="dve")
            else:
                d1 = P.op("dve", lambda v, w=w: v.reduce_max(
                    kmax2[0:1, 2:3], pkm[0:1, 0:w], axis=AX.X), sem="dve")
                P.wait("dve", "dve", d1)
                P.op("dve", lambda v: v.tensor_max(
                    kmax2[0:1, 0:1], kmax2[0:1, 0:1], kmax2[0:1, 2:3]),
                    sem="dve")
            pkm_dve[0] = P.cnt["dve"]
            first_o = False
        dve_km = P.cnt["dve"]

        qtiles = []
        for o, w in [(o, min(512, S - o)) for o in range(0, S, 512)]:
            bi = next_pp()
            for qc in range(QC):
                sem = "pe" if qc == QC - 1 else None
                P.op("pe", lambda t, bi=bi, qc=qc, o=o, w=w, b=b: t.matmul(
                    pprep[bi][:, 0:w],
                    wqbn_sb[b][:, qc * 128:(qc + 1) * 128],
                    qn[:, qc * S + o:qc * S + o + w],
                    start=(qc == 0), stop=(qc == QC - 1)), sem=sem)
            pe_mm = P.cnt["pe"]
            P.wait("act", "pe", pe_mm)
            a1 = P.op("act", lambda s, bi=bi, o=o, w=w, b=b: s.activation(
                q_nope[b][:, o:o + w], pprep[bi][:, 0:w], AF.Copy), sem="act")
            P.wait("act", "act", a1)
            P.wait("act", "pe", sqh_pe_free[0])
            a2 = P.op("act", lambda s, bi=bi, w=w, o=o: s.activation(
                sq_h[:, o:o + w], pprep[bi][:, 0:w], AF.Square), sem="act")
            prep_psum_free[bi] = (a2, P.cnt["dve"])
            qtiles.append((o, w, a2))
        P.wait("act", "dve", dve_rope)
        a3 = P.op("act", lambda s, b=b: s.activation(
            sq_qr[0:64, :], q_aug[b][0:64, :], AF.Square), sem="act")
        P.wait("pe", "act", pkm_act_h[0])
        last_ev = 0
        for o, w, a2 in qtiles:
            P.wait("pe", "act", a2)
            P.wait("pe", "act", a3)
            P.wait("pe", "dve", pkm_dve[0])
            P.op("pe", lambda t, w=w, o=o: t.matmul(
                pkm[0:1, 0:w], ones_col_bf[:, :], sq_h[:, o:o + w],
                start=True, stop=False), sem=None)
            pm = P.op("pe", lambda t, w=w, o=o: t.matmul(
                pkm[0:1, 0:w], ones_col_bf[0:64, :], sq_qr[0:64, o:o + w],
                start=False, stop=True), sem="pe")
            sqh_pe_free[0] = pm
            P.wait("dve", "pe", pm)
            last_ev = P.op("dve", lambda v, o=o, w=w: v.tensor_copy(
                qss_row[0:1, o:o + w], pkm[0:1, 0:w]), sem="dve")
            pkm_dve[0] = last_ev
        P.wait("act", "dve", last_ev)
        P.wait("act", "dve", dve_km)
        ab = P.op("act", lambda s, b=b: s.activation(
            q_aug[b][64:65, :], qss_row[0:1, :], AF.Sqrt,
            scale=kmax2[0:1, 0:1]), sem="act")
        pkm_act_h[0] = 0
        head_prep_evac[h] = (P.cnt["act"], P.cnt["dve"])

    kmax_act_read = [0]

    def main(h):
        b = h % 2
        P.wait("pe", "act", head_prep_evac[h][0])
        P.wait("pe", "dve", head_prep_evac[h][1])
        for s_ in range(cfg.NSTRIP):
            l0 = s_ * WL
            mt_hi = (s_ + 1) * (WL // 128)

            def emit_scores(mt):
                o = max(0, mt * 128 - l0)
                sbi = mt % 2
                fr = scores_free.get(sbi)
                if fr is not None:
                    P.wait("pe", "act", fr)
                for oo, w in tile512(WL - o):
                    P.op("pe", lambda t, sbi=sbi, mt=mt, o=o, oo=oo, w=w, b=b,
                         l0=l0: t.matmul(
                             psb[sbi][:, o + oo:o + oo + w],
                             k_nope[b][:, mt * 128:(mt + 1) * 128],
                             q_nope[b][:, l0 + o + oo:l0 + o + oo + w],
                             start=True, stop=False), sem=None)
                    sem = "pe" if oo + w == WL - o else None
                    P.op("pe", lambda t, sbi=sbi, mt=mt, o=o, oo=oo, w=w, b=b,
                         l0=l0: t.matmul(
                             psb[sbi][:, o + oo:o + oo + w],
                             kra[0:65, mt * 128:(mt + 1) * 128],
                             q_aug[b][0:65, l0 + o + oo:l0 + o + oo + w],
                             start=False, stop=True), sem=sem)
                pe_sc = P.cnt["pe"]
                dve_mask = None
                if mt * 128 >= l0:
                    P.wait("dve", "pe", pe_sc)
                    dve_mask = P.op("dve", lambda v, sbi=sbi, o=o: v.tensor_add(
                        psb[sbi][:, o:o + 128], psb[sbi][:, o:o + 128],
                        maskT[:, :]), sem="dve")
                P.wait("act", "pe", pe_sc)
                if dve_mask:
                    P.wait("act", "dve", dve_mask)
                pfr = probsT_free.get(sbi)
                if pfr is not None:
                    P.wait("act", "pe", pfr)
                a_exp = P.op("act", lambda s, sbi=sbi, o=o: s.activation(
                    probsT[sbi][:, 0:WL - o], psb[sbi][:, o:WL], AF.Exp,
                    scale=sc192[:, 0:1]), sem="act")
                scores_free[sbi] = a_exp
                return o, sbi, a_exp

            def emit_av(mt, o, sbi, a_exp):
                P.wait("pe", "act", a_exp)
                if mt == 0:
                    P.wait("pe", "dve", st_avrs["pav"])
                    P.wait("pe", "dve", st_avrs["prs"])
                for oo, w in tile512(WL - o):
                    P.op("pe", lambda t, sbi=sbi, mt=mt, o=o, oo=oo, w=w, b=b,
                         hi=mt_hi: t.matmul(
                             pav[:, o + oo:o + oo + w],
                             v_tm[b][:, mt * 128:(mt + 1) * 128],
                             probsT[sbi][:, oo:oo + w],
                             start=(mt == 0), stop=(mt == hi - 1)), sem=None)
                    sem = "pe" if oo + w == WL - o else None
                    P.op("pe", lambda t, sbi=sbi, mt=mt, o=o, oo=oo, w=w,
                         hi=mt_hi: t.matmul(
                             prs[0:1, o + oo:o + oo + w], ones_col_bf[:, :],
                             probsT[sbi][:, oo:oo + w],
                             start=(mt == 0), stop=(mt == hi - 1)), sem=sem)
                probsT_free[sbi] = P.cnt["pe"]

            pend = None
            for mt in range(mt_hi):
                info = emit_scores(mt)
                if mt == min(1, mt_hi - 1) and pend_fin[0] is not None:
                    pend_fin[0]()
                    pend_fin[0] = None
                if pend is not None:
                    emit_av(*pend)
                pend = (mt,) + info
            emit_av(*pend)
            pe_avdone = P.cnt["pe"]
            P.wait("dve", "pe", pe_avdone)
            def _recip(v):
                with nc.allow_low_precision(
                        reason="softmax denom reciprocal consumed as bf16 "
                               "matmul operand (bf16-honest)"):
                    return v.reciprocal(rr_bf[:, :], prs[0:1, :])
            d2 = P.op("dve", _recip, sem="dve")
            st_avrs["prs"] = d2

            def finalize(d2=d2, h=h, l0=l0, s_=s_):
                P.wait("pe", "dve", d2)
                P.wait("pe", "act", st_avrs["prb"])
                for oo, w in tile512(WL):
                    sem = "pe" if oo + w == WL else None
                    P.op("pe", lambda t, oo=oo, w=w: t.matmul(
                        prb[:, oo:oo + w], ones_row_bf[:, :],
                        rr_bf[0:1, oo:oo + w], start=True, stop=True), sem=sem)
                pe_bc = P.cnt["pe"]
                P.wait("act", "pe", pe_bc)
                a2 = P.op("act", lambda s: s.activation(rb_sb[:, :], prb[:, :],
                                                        AF.Copy), sem="act")
                st_avrs["prb"] = a2
                osb = (h * cfg.NSTRIP + s_) % 2
                P.wait("dve", "act", a2)
                ofr = ostrip_free.get(osb)
                if ofr is not None:
                    P.wait("dve", "os%d" % osb, ofr)
                d3 = P.op("dve", lambda v, osb=osb: v.tensor_mul(
                    ostrip[osb][:, :], pav[:, :], rb_sb[:, :]), sem="dve")
                st_avrs["pav"] = d3
                P.wait("sp", "dve", d3)
                so1 = P.op("sp", lambda e, h=h, l0=l0, osb=osb: e.dma_start(
                    oh_hbm[h, :, l0:l0 + WL], ostrip[osb][:, :]),
                    sem="os%d" % osb, inc=16)
                ostrip_free[osb] = so1
            pend_fin[0] = finalize
        if pend_fin[0] is not None:
            pend_fin[0]()
            pend_fin[0] = None
        head_main_done[h] = (P.cnt["pe"], P.cnt["dve"])

    head_loads(0)
    head_loads(1)
    for h in range(H):
        prep(h)
        if h + 2 < H:
            head_loads(h + 2)
        main(h)
        if h == 0:
            P.dump("q_nope0", q_nope[0][:, :], [128, S], BF16)
            P.dump("q_aug0", q_aug[0][0:65, :], [65, S], BF16)
            P.dump("k_nope0", k_nope[0][:, :], [128, S], BF16)
            P.dump("v_tm0", v_tm[0][:, :], [128, S], BF16)

    if cfg.stop_after == "A3":
        early_out()
        return nc, P

    P.barrier()
    nc.sbuf_base = sb_pers
    nc.psum_base = 0

    oa = sb("oa", [128, H * S], BF16)
    wo_sb = [sb("wo_sb0", [128, H * 128], BF16),
             sb("wo_sb1", [128, H * 128], BF16)]
    x_sb = [sb("x_sb0", [128, S], F32), sb("x_sb1", [128, S], F32)]
    x1p = [sb("x1p0", [128, TH], F32), sb("x1p1", [128, TH], F32)]
    x1p_bf = [sb("x1p_bf0", [128, TH], BF16), sb("x1p_bf1", [128, TH], BF16)]
    sq4 = [sb("sq40", [128, TH], BF16), sb("sq41", [128, TH], BF16)]
    mu_row = sb("mu_row", [1, S], F32)
    mu2_row = sb("mu2_row", [1, S], F32)
    rstd1_row = sb("rstd1_row", [1, S], F32)
    nmu_row = sb("nmu_row", [1, S], F32)
    csum_row = sb("csum_row", [1, E], F32)
    colsum_b = sb("colsum_b", [128, E], F32)
    rstd_col = sb("rstd_col", [128, NT], F32)
    nmu_col = sb("nmu_col", [128, NT], F32)
    aff_f = sb("aff_f", [128, NT * E], F32)
    aff_s = sb("aff_s", [128, NT * E], F32)
    m8 = sb("m8", [128, 8], F32)
    s12 = sb("s12", [128, 2], F32)
    cm = sb("cm", [128, E], F32)
    assert nc.sbuf_base <= nc.sbuf_top, f"A4 sbuf over {nc.sbuf_base}"

    pwo = [psum("pwo0", [128, TH]), psum("pwo1", [128, TH])]
    pstm = [psum("pstm0", [1, 512]), psum("pstm1", [1, 512])]
    paff = psum("paff", [128, NT * E + E])
    pmisc = psum("pmisc", [128, 256])
    assert nc.psum_base <= 8, f"A4 psum over {nc.psum_base}"

    P.wait("sp", "os0", P.cnt["os0"])
    P.wait("sp", "os1", P.cnt["os1"])
    for cd in range(H):
        P.op("sp", lambda e, cd=cd: e.dma_start(
            oa[:, cd * S:(cd + 1) * S], oh_hbm[cd]), sem="sx", inc=16)
    sx_oa = P.cnt["sx"]

    aff_started = [False]
    pend4 = []
    wo_free = {}
    wo_cnt2 = [0, 0]
    x_cnt = [0, 0]
    x1p_war = {}
    pwo_free = {}
    row_dve = {}
    sq4_pe = [0, 0]
    x1pbf_pe = [0, 0]

    for dc in range(CD):
        sl = dc % 2
        P.wait("pool", "pe", wo_free.get(sl, 0))
        wo_cnt2[sl] += 1
        whd_cnt[sl] += 1
        P.op("pool", lambda g, dc=dc, sl=sl: g.dma_start(
            wo_sb[sl][:, :], wo_re[dc]), sem="wh%d" % sl, inc=16)
        P.wait("sp", "dve", x1p_war.get(("x", sl), 0))
        x_cnt[sl] += 1
        P.op("sp", lambda e, dc=dc, sl=sl: e.dma_start(
            x_sb[sl][:, :], x_fm[dc * 128:(dc + 1) * 128, :]),
            sem="sx%d" % sl, inc=16)
        for th in range(2):
            ts0 = th * TH
            bi = (dc * 2 + th) % 2
            P.wait("pe", "wh%d" % sl, whd_cnt[sl] * 16)
            P.wait("pe", "sx", sx_oa)
            if bi in pwo_free:
                P.wait("pe", "dve", pwo_free[bi])
            for cd in range(H):
                for o, w in tile512(TH):
                    sem = "pe" if (cd == H - 1 and o + w == TH) else None
                    P.op("pe", lambda t, bi=bi, cd=cd, o=o, w=w, sl=sl, ts0=ts0:
                         t.matmul(pwo[bi][:, o:o + w],
                                  wo_sb[sl][:, cd * 128:(cd + 1) * 128],
                                  oa[:, cd * S + ts0 + o:cd * S + ts0 + o + w],
                                  start=(cd == 0), stop=(cd == H - 1)), sem=sem)
            pe_wo = P.cnt["pe"]
            wo_free[sl] = pe_wo
            P.wait("dve", "sx%d" % sl, x_cnt[sl] * 16)
            P.wait("dve", "pe", pe_wo)
            P.wait("dve", "os%d" % bi, x1p_war.get(("st", bi), 0))
            P.wait("dve", "pe", x1p_war.get(("aff", bi), 0))
            d_res = P.op("dve", lambda v, bi=bi, sl=sl, ts0=ts0: v.tensor_add(
                x1p[bi][:, :], pwo[bi][:, :], x_sb[sl][:, ts0:ts0 + TH]),
                sem="dve")
            pwo_free[bi] = d_res
            x1p_war[("x", sl)] = d_res
            P.wait("sp", "dve", d_res)
            so_x1p = P.op("sp", lambda e, dc=dc, ts0=ts0, bi=bi: e.dma_start(
                x1pre_hbm[dc * 128:(dc + 1) * 128, ts0:ts0 + TH], x1p[bi][:, :]),
                sem="os%d" % bi, inc=16)
            x1p_war[("st", bi)] = so_x1p
            gb_ = (dc * 2 + th) % 2
            P.wait("act", "dve", d_res)
            P.wait("act", "pe", sq4_pe[gb_])
            a_sq = P.op("act", lambda s, bi=bi, gb_=gb_: s.activation(
                sq4[gb_][:, :], x1p[bi][:, :], AF.Square), sem="act")
            P.wait("dve", "pe", x1pbf_pe[gb_])
            P.wait("dve", "dve", d_res)
            d_bf = P.op("dve", lambda v, bi=bi, gb_=gb_: v.tensor_copy(
                x1p_bf[gb_][:, :], x1p[bi][:, :]), sem="dve")

            def emit_stats4(dc=dc, th=th, ts0=ts0, gb_=gb_, a_sq=a_sq,
                            d_bf=d_bf):
                for o, w in tile512(TH):
                    P.wait("pe", "act", a_sq)
                    P.wait("pe", "dve", d_bf)
                    P.wait("pe", "dve", row_dve.get("pstm", 0))
                    P.op("pe", lambda t, o=o, w=w: t.matmul(
                        pstm[0][0:1, 0:w], ones_col_bf[:, :],
                        x1p_bf[gb_][:, o:o + w], start=True, stop=True),
                        sem=None)
                    p2 = P.op("pe", lambda t, o=o, w=w: t.matmul(
                        pstm[1][0:1, 0:w], ones_col_bf[:, :],
                        sq4[gb_][:, o:o + w], start=True, stop=True), sem="pe")
                    P.wait("dve", "pe", p2)
                    if dc == 0:
                        P.op("dve", lambda v, o=o, w=w, ts0=ts0: v.tensor_copy(
                            mu_row[0:1, ts0 + o:ts0 + o + w],
                            pstm[0][0:1, 0:w]), sem="dve")
                        d2 = P.op("dve", lambda v, o=o, w=w, ts0=ts0:
                                  v.tensor_copy(
                                      mu2_row[0:1, ts0 + o:ts0 + o + w],
                                      pstm[1][0:1, 0:w]), sem="dve")
                    else:
                        P.op("dve", lambda v, o=o, w=w, ts0=ts0: v.tensor_add(
                            mu_row[0:1, ts0 + o:ts0 + o + w],
                            mu_row[0:1, ts0 + o:ts0 + o + w],
                            pstm[0][0:1, 0:w]), sem="dve")
                        d2 = P.op("dve", lambda v, o=o, w=w, ts0=ts0:
                                  v.tensor_add(
                                      mu2_row[0:1, ts0 + o:ts0 + o + w],
                                      mu2_row[0:1, ts0 + o:ts0 + o + w],
                                      pstm[1][0:1, 0:w]), sem="dve")
                    row_dve["pstm"] = d2
                sq4_pe[gb_] = P.cnt["pe"]
                x1pbf_pe[gb_] = P.cnt["pe"]
            pend4.append(emit_stats4)
            if len(pend4) > 1:
                pend4.pop(0)()
            P.wait("pe", "dve", d_res)
            for tti in range(TH // 128):
                tglob = th * (TH // 128) + tti
                st = not aff_started[0]
                aff_started[0] = True
                P.op("pe", lambda t, tti=tti, tglob=tglob, bi=bi, dc=dc, st=st:
                     t.matmul(paff[:, tglob * E:(tglob + 1) * E],
                              x1p[bi][:, tti * 128:(tti + 1) * 128],
                              wa_sb[:, dc * E:(dc + 1) * E],
                              start=st, stop=False, skip_group_check=True),
                     sem=None)
            if th == 0:
                P.op("pe", lambda t, dc=dc: t.matmul(
                    paff[0:1, NT * E:NT * E + E], ones_col_f[:, :],
                    wa_sb[:, dc * E:(dc + 1) * E],
                    start=False, stop=False, skip_group_check=True), sem=None)
            x1p_war[("aff", bi)] = P.cnt["pe"]
    while pend4:
        pend4.pop(0)()
    P.op("pe", lambda t: t.matmul(
        paff[0:1, NT * E:NT * E + E], ones_col_f[0:1, :], zrow[0:1, 0:E],
        start=False, stop=True, skip_group_check=True), sem="pe")
    pe_aff_all = P.cnt["pe"]

    P.wait("dve", "pe", pe_aff_all)
    P.wait("dve", "dve", P.cnt["dve"])
    f1 = P.op("dve", lambda v: v.tensor_scalar_mul(mu_row[:, :], mu_row[:, :],
                                                   1.0 / D), sem="dve")
    f2 = P.op("dve", lambda v: v.tensor_scalar_mul(mu2_row[:, :], mu2_row[:, :],
                                                   1.0 / D), sem="dve")
    P.wait("dve", "dve", f2)
    f3 = P.op("dve", lambda v: v.tensor_mul(rstd1_row[:, :], mu_row[:, :],
                                            mu_row[:, :]), sem="dve")
    P.wait("dve", "dve", f3)
    f4 = P.op("dve", lambda v: v.tensor_sub(mu2_row[:, :], mu2_row[:, :],
                                            rstd1_row[:, :]), sem="dve")
    P.wait("act", "dve", f4)
    f5 = P.op("act", lambda s: s.activation(rstd1_row[:, :], mu2_row[:, :],
                                            AF.Sqrt, bias=fconst[0:1, 3:4]),
              sem="act")
    P.wait("dve", "act", f5)
    f6 = P.op("dve", lambda v: v.reciprocal(rstd1_row[:, :], rstd1_row[:, :]),
              sem="dve")
    P.wait("dve", "dve", f6)
    f7 = P.op("dve", lambda v: v.tensor_mul(nmu_row[:, :], mu_row[:, :],
                                            rstd1_row[:, :]), sem="dve")
    P.wait("dve", "dve", f7)
    f8 = P.op("dve", lambda v: v.tensor_scalar_mul(nmu_row[:, :], nmu_row[:, :],
                                                   -1.0), sem="dve")
    dve_rows = P.cnt["dve"]

    P.wait("pe", "dve", dve_rows)
    P.wait("pe", "sx", sx_consts)
    for tti in range(NT):
        P.op("pe", lambda t, tti=tti: t.transpose(
            pmisc[:, 0:1], rstd1_row[0:1, tti * 128:(tti + 1) * 128],
            ident[0:1, 0:1]), sem=None)
        p2 = P.op("pe", lambda t, tti=tti: t.transpose(
            pmisc[:, 1:2], nmu_row[0:1, tti * 128:(tti + 1) * 128],
            ident[0:1, 0:1]), sem="pe")
        P.wait("dve", "pe", p2)
        P.op("dve", lambda v, tti=tti: v.tensor_copy(
            rstd_col[:, tti:tti + 1], pmisc[:, 0:1]), sem="dve")
        d2 = P.op("dve", lambda v, tti=tti: v.tensor_copy(
            nmu_col[:, tti:tti + 1], pmisc[:, 1:2]), sem="dve")
        P.wait("pe", "dve", d2)
    dve_cols = P.cnt["dve"]

    P.wait("dve", "pe", pe_aff_all)
    dcs = P.op("dve", lambda v: v.tensor_copy(
        csum_row[:, :], paff[0:1, NT * E:NT * E + E]), sem="dve")
    P.wait("pe", "dve", dcs)
    pc1 = P.op("pe", lambda t: t.matmul(pmisc[:, 16:16 + E], ones_row_f[:, :],
                                        csum_row[0:1, :], start=True, stop=True),
               sem="pe")
    P.wait("dve", "pe", pc1)
    dcb = P.op("dve", lambda v: v.tensor_copy(colsum_b[:, :],
                                              pmisc[:, 16:16 + E]), sem="dve")

    P.wait("dve", "dve", dve_cols)
    P.wait("dve", "dve", dcb)
    for tti in range(NT):
        t1 = P.op("dve", lambda v, tti=tti: v.tensor_scalar_mul(
            aff_f[:, tti * E:(tti + 1) * E], paff[:, tti * E:(tti + 1) * E],
            rstd_col[:, tti:tti + 1]), sem="dve")
        P.wait("dve", "dve", t1)
        P.op("dve", lambda v, tti=tti: v.scalar_tensor_tensor(
            aff_f[:, tti * E:(tti + 1) * E], colsum_b[:, :],
            nmu_col[:, tti:tti + 1], aff_f[:, tti * E:(tti + 1) * E],
            ALU.mult, ALU.add), sem="dve")
    dve_afff = P.cnt["dve"]
    P.wait("act", "dve", dve_afff)
    a_sig = P.op("act", lambda s: s.activation(aff_s[:, :], aff_f[:, :],
                                               AF.Sigmoid), sem="act")
    P.wait("dve", "act", a_sig)
    for tti in range(NT):
        g1 = P.op("dve", lambda v, tti=tti: v.max(
            out=m8[:, :], in_=aff_s[:, tti * E:(tti + 1) * E]), sem="dve")
        P.wait("dve", "dve", g1)
        g2 = P.op("dve", lambda v: v.tensor_add(s12[:, 0:1], m8[:, 0:1],
                                                m8[:, 1:2]), sem="dve")
        P.wait("dve", "dve", g2)
        g3 = P.op("dve", lambda v: v.reciprocal(s12[:, 1:2], s12[:, 0:1]),
                  sem="dve")
        g4 = P.op("dve", lambda v, tti=tti: v.tensor_scalar(
            cm[:, :], aff_s[:, tti * E:(tti + 1) * E], m8[:, 1:2], None,
            ALU.is_ge), sem="dve")
        P.wait("dve", "dve", g4)
        g5 = P.op("dve", lambda v, tti=tti: v.tensor_mul(
            cm[:, :], cm[:, :], aff_s[:, tti * E:(tti + 1) * E]), sem="dve")
        P.wait("dve", "dve", g5)
        g6 = P.op("dve", lambda v: v.tensor_scalar_mul(cm[:, :], cm[:, :],
                                                       s12[:, 1:2]), sem="dve")
        P.wait("pe", "dve", g6)
        p1 = P.op("pe", lambda t: t.transpose(pmisc[0:E, 32:32 + 128], cm[:, :],
                                              ident[:, :]), sem="pe")
        P.wait("dve", "pe", p1)
        g7 = P.op("dve", lambda v, tti=tti: v.tensor_copy(
            c_bf[:, tti * 128:(tti + 1) * 128], pmisc[0:E, 32:32 + 128]),
            sem="dve")
        P.wait("pe", "dve", g7)
    P.dump("aff_s", aff_s[:, :], [128, NT * E])
    P.dump("c_bf", c_bf[:, :], [E, S], BF16)
    P.dump("mu_row", mu_row[:, :], [1, S])
    P.dump("rstd1_row", rstd1_row[:, :], [1, S])

    if cfg.stop_after == "A4":
        early_out()
        return nc, P

    a4_end = dict(P.cnt)
    P.barrier()
    nc.psum_base = 0
    pmu = psum("pmu", [128, S])
    prst = psum("prst", [128, S])
    assert nc.psum_base <= 8
    sb_a4b = nc.sbuf_base
    x1n = [sb("x1n0", [128, S], F32), sb("x1n1", [128, S], F32)]
    x1t = sb("x1t", [128, S], F32)
    assert nc.sbuf_base <= nc.sbuf_top

    bcast_row_f32(mu_row, pmu, S, [])
    pe_bc3 = bcast_row_f32(rstd1_row, prst, S, [])
    x1n_war = {}
    x1t_war = [0]
    for dc in range(CD):
        sl = dc % 2
        P.wait("sp", "dve", x1n_war.get(("ld", sl), 0))
        P.wait("sp", "os%d" % sl, x1n_war.get(("st", sl), 0))
        x_cnt[sl] += 1
        P.op("sp", lambda e, dc=dc, sl=sl: e.dma_start(
            x1n[sl][:, :], x1pre_hbm[dc * 128:(dc + 1) * 128, :]),
            sem="sx%d" % sl, inc=16)
        P.wait("dve", "sx%d" % sl, x_cnt[sl] * 16)
        P.wait("dve", "pe", pe_bc3)
        P.wait("dve", "dve", x1t_war[0])
        d1 = P.op("dve", lambda v, sl=sl: v.tensor_sub(
            x1t[:, :], x1n[sl][:, :], pmu[:, :]), sem="dve")
        P.wait("dve", "dve", d1)
        P.wait("dve", "os%d" % sl, x1n_war.get(("st", sl), 0))
        d2 = P.op("dve", lambda v, sl=sl: v.tensor_mul(
            x1n[sl][:, :], x1t[:, :], prst[:, :]), sem="dve")
        x1n_war[("ld", sl)] = d2
        x1t_war[0] = d2
        P.wait("sp", "dve", d2)
        so1 = P.op("sp", lambda e, dc=dc, sl=sl: e.dma_start(
            x1_hbm[dc * 128:(dc + 1) * 128, :], x1n[sl][:, :]),
            sem="os%d" % sl, inc=16)
        x1n_war[("st", sl)] = so1
    so_x1_all = (P.cnt["os0"], P.cnt["os1"])
    P.dump("x1", x1n[(CD - 1) % 2][:, :], [128, S])

    if cfg.stop_after == "A4b":
        early_out()
        return nc, P

    if cfg.safe_barriers in ("both", "f"):
        P.barrier()
    dve_a4b_end = P.cnt["dve"]
    nc.sbuf_base = sb_pers
    nc.psum_base = 0
    b2r_bf = sb("b2r_bf", [E, D], BF16)
    esel = sb("esel", [E, E * 128], BF16)
    if cfg.safe_barriers == "f2":
        for nm_ in ("pe", "act", "dve", "os0", "os1", "sx", "sx0", "sx1",
                    "wh0", "wh1", "cst", "xbf", "so", "pool"):
            P.wait("pool", nm_, P.cnt[nm_])
    else:
        P.wait("pool", "pe", a4_end["pe"])
        P.wait("pool", "act", a4_end["act"])
        P.wait("pool", "dve", a4_end["dve"])
    P.op("pool", lambda g: g.dma_start(b2r_bf[:, :], b2r_re), sem="cst", inc=16)
    P.op("pool", lambda g: g.dma_start(esel[:, :], esel_re), sem="cst", inc=16)
    cst_f = P.cnt["cst"]
    x1b = sb("x1b", [128, CD * TT], BF16)
    wr_sb = [sb("wr_sb0", [128, CD * F2], BF16),
             sb("wr_sb1", [128, CD * F2], BF16)]
    g_act = sb("g_act", [128, FC * TT], BF16)
    acc = sb("acc", [128, CD * TT], F32)
    c_bc = [sb("c_bc0", [128, TT], BF16)]
    x1r = [sb("x1r0", [128, TT], F32), sb("x1r1", [128, TT], F32)]
    sqf = [sb("sqf0", [128, TT], BF16), sb("sqf1", [128, TT], BF16)]
    accbf = [sb("accbf0", [128, TT], BF16), sb("accbf1", [128, TT], BF16)]
    muf_row = sb("muf_row", [1, TT], F32)
    mu2f_row = sb("mu2f_row", [1, TT], F32)
    tmpf_row = sb("tmpf_row", [1, TT], F32)
    of_sb = [sb("of_sb0", [128, TT], F32), sb("of_sb1", [128, TT], F32)]
    assert nc.sbuf_base <= nc.sbuf_top, f"F sbuf over {nc.sbuf_base}"

    pg = [psum("pg0", [128, TT]), psum("pg1", [128, TT])]
    po = [psum("po0", [128, TT]), psum("po1", [128, TT])]
    assert nc.psum_base <= 8

    wr_cnt = [0, 0]
    wr_free = {}
    ln_war = {}
    of_war = {}
    row2_dve = {}
    ln2_dve_end = [0]
    gact_w2_pe = [0]
    gact_g_act = {}
    gact_y_dve = {}
    pg_free = {}
    po_free = {}
    cbc_free = {}
    x1b_pe_done = [0]

    def slot_src(e, k):
        routed = e < E
        if k < 2:
            return rg_re[e, k] if routed else sg_re[e - E, k]
        if k < 4:
            return rw1_re[e, k - 2] if routed else sw1_re[e - E, k - 2]
        return rw2_re[e, k - 4] if routed else sw2_re[e - E, k - 4]

    def load_slot(e, k, gate_pe):
        sl = (e * 6 + k) % 2
        P.wait("pool", "pe", max(gate_pe, wr_free.get(sl, 0)))
        wr_cnt[sl] += 1
        P.op("pool", lambda g, sl=sl, src=slot_src(e, k): g.dma_start(
            wr_sb[sl][:, :], src), sem="wr%d" % sl, inc=16)
        return sl, wr_cnt[sl]

    for tt in range(2):
        ts0 = tt * TT
        acc_dve = {}
        phase_end = {}
        slot_q = {}
        slot_q[(0, 0)] = load_slot(0, 0, 0)
        slot_q[(0, 1)] = load_slot(0, 1, 0)
        P.wait("pool", "os0", so_x1_all[0])
        P.wait("pool", "os1", so_x1_all[1])
        if tt == 1:
            P.wait("pool", "pe", x1b_pe_done[0])
        for dc in range(CD):
            P.op("pool", lambda g, dc=dc, ts0=ts0: g.dma_start(
                x1b[:, dc * TT:(dc + 1) * TT],
                x1_hbm[dc * 128:(dc + 1) * 128, ts0:ts0 + TT]),
                sem="xbf", inc=16)
        xbf_all = P.cnt["xbf"]
        if tt == 1:
            P.wait("dve", "act", P.cnt["act"])
        for e in range(NE):
            routed = e < E
            cbi = 0
            if routed:
                P.wait("pe", "cst", cst_f)
                if e == 0 and tt == 0:
                    P.wait("pe", "dve", dve_a4b_end)
                fr = pg_free.get(0)
                if fr is not None:
                    P.wait("pe", "act", fr[0])
                    P.wait("pe", "dve", fr[1])
                for o, w in tile512(TT):
                    sem = "pe" if o + w == TT else None
                    P.op("pe", lambda t, e=e, o=o, w=w, ts0=ts0: t.matmul(
                        pg[0][:, o:o + w], esel[:, e * 128:(e + 1) * 128],
                        c_bf[:, ts0 + o:ts0 + o + w], start=True, stop=True),
                        sem=sem)
                pe_cbc = P.cnt["pe"]
                P.wait("act", "pe", pe_cbc)
                P.wait("act", "dve", cbc_free.get(cbi, 0))
                if e == 0 and tt == 0:
                    P.wait("act", "os0", so_x1_all[0])
                    P.wait("act", "os1", so_x1_all[1])
                a_cbc = P.op("act", lambda s, cbi=cbi: s.activation(
                    c_bc[cbi][:, :], pg[0][:, :], AF.Copy), sem="act")
                pg_free[0] = (a_cbc, pg_free.get(0, (0, 0))[1])
            else:
                a_cbc = None
            P.wait("pe", "xbf", xbf_all)
            for k in range(6):
                if (e, k) not in slot_q:
                    slot_q[(e, k)] = load_slot(e, k, phase_end.get((e, k - 2), 0)
                                               if k >= 2 else
                                               phase_end.get((e - 1, k + 4), 0))
                sl, cnt = slot_q[(e, k)]
                P.wait("pe", "wr%d" % sl, cnt * 16)
                if k < 4:
                    half = k % 2
                    isG = k < 2
                    for jj in range(FC // 2):
                        fc = half * (FC // 2) + jj
                        bi = fc % 2
                        fr = pg_free.get(bi)
                        if fr is not None:
                            P.wait("pe", "act", fr[0])
                            P.wait("pe", "dve", fr[1])
                        for dcc in range(CD):
                            for o, w in tile512(TT):
                                sem = "pe" if (dcc == CD - 1 and o + w == TT) \
                                    else None
                                P.op("pe", lambda t, bi=bi, sl=sl, dcc=dcc,
                                     jj=jj, o=o, w=w: t.matmul(
                                         pg[bi][:, o:o + w],
                                         wr_sb[sl][:, dcc * F2 + jj * 128:
                                                   dcc * F2 + jj * 128 + 128],
                                         x1b[:, dcc * TT + o:dcc * TT + o + w],
                                         start=(dcc == 0), stop=(dcc == CD - 1)),
                                     sem=sem)
                        pe_p = P.cnt["pe"]
                        if isG:
                            P.wait("act", "pe", pe_p)
                            P.wait("act", "pe", gact_w2_pe[0])
                            a_g = P.op("act", lambda s, bi=bi, e=e, fc=fc:
                                       s.activation(
                                           g_act[:, fc * TT:(fc + 1) * TT],
                                           pg[bi][:, :], AF.Gelu,
                                           bias=gb_sb[:, e * FC + fc:
                                                      e * FC + fc + 1]),
                                       sem="act")
                            pg_free[bi] = (a_g, pg_free.get(bi, (0, 0))[1])
                            gact_g_act[fc] = a_g
                        else:
                            P.wait("dve", "pe", pe_p)
                            P.wait("dve", "act", gact_g_act[fc])
                            d_y = P.op("dve", lambda v, bi=bi, e=e, fc=fc:
                                       v.scalar_tensor_tensor(
                                           g_act[:, fc * TT:(fc + 1) * TT],
                                           pg[bi][:, :],
                                           b1_sb[:, e * FC + fc:
                                                 e * FC + fc + 1],
                                           g_act[:, fc * TT:(fc + 1) * TT],
                                           ALU.add, ALU.mult), sem="dve")
                            if routed:
                                P.wait("dve", "dve", d_y)
                                P.wait("dve", "act", a_cbc)
                                d_y = P.op("dve", lambda v, fc=fc, cbi=cbi:
                                           v.tensor_mul(
                                               g_act[:, fc * TT:(fc + 1) * TT],
                                               g_act[:, fc * TT:(fc + 1) * TT],
                                               c_bc[cbi][:, :]), sem="dve")
                            pg_free[bi] = (pg_free.get(bi, (0, 0))[0], d_y)
                            gact_y_dve[fc] = d_y
                else:
                    half = k - 4
                    for jj in range(CD // 2):
                        dco = half * (CD // 2) + jj
                        bi = dco % 2
                        P.wait("pe", "dve", po_free.get(bi, 0))
                        seeded = False
                        if e == 0:
                            for o, w in tile512(TT):
                                P.op("pe", lambda t, bi=bi, dco=dco, o=o, w=w,
                                     ts0=ts0: t.matmul(
                                         po[bi][:, o:o + w],
                                         b2r_bf[:, dco * 128:dco * 128 + 128],
                                         c_bf[:, ts0 + o:ts0 + o + w],
                                         start=True, stop=False), sem=None)
                            seeded = True
                        for fc in range(FC):
                            if jj == 0:
                                P.wait("pe", "dve", gact_y_dve[fc])
                            for o, w in tile512(TT):
                                sem = "pe" if (fc == FC - 1 and o + w == TT) \
                                    else None
                                st = (fc == 0) and not seeded
                                P.op("pe", lambda t, bi=bi, sl=sl, fc=fc, jj=jj,
                                     o=o, w=w, st=st: t.matmul(
                                         po[bi][:, o:o + w],
                                         wr_sb[sl][:, fc * D2 + jj * 128:
                                                   fc * D2 + jj * 128 + 128],
                                         g_act[:, fc * TT + o:fc * TT + o + w],
                                         start=st, stop=(fc == FC - 1)),
                                     sem=sem)
                        pe_w2 = P.cnt["pe"]
                        gact_w2_pe[0] = pe_w2
                        P.wait("dve", "pe", pe_w2)
                        if e == 0:
                            P.wait("dve", "dve", ln2_dve_end[0])
                            d_a = P.op("dve", lambda v, bi=bi, dco=dco:
                                       v.tensor_scalar_add(
                                           acc[:, dco * TT:(dco + 1) * TT],
                                           po[bi][:, :], ssb2[:, dco:dco + 1]),
                                       sem="dve")
                        else:
                            P.wait("dve", "dve", acc_dve.get(dco, 0))
                            d_a = P.op("dve", lambda v, bi=bi, dco=dco:
                                       v.tensor_add(
                                           acc[:, dco * TT:(dco + 1) * TT],
                                           acc[:, dco * TT:(dco + 1) * TT],
                                           po[bi][:, :]), sem="dve")
                        acc_dve[dco] = d_a
                        po_free[bi] = d_a
                wr_free[sl] = P.cnt["pe"]
                phase_end[(e, k)] = P.cnt["pe"]
                if k == 5 and e + 1 < NE:
                    slot_q[(e + 1, 0)] = load_slot(e + 1, 0, phase_end[(e, 4)])
                    slot_q[(e + 1, 1)] = load_slot(e + 1, 1, phase_end[(e, 5)])
            x1b_pe_done[0] = P.cnt["pe"]
            if routed:
                cbc_free[cbi] = gact_y_dve[FC - 1]
        st_pe = [0, 0]
        pendf = []
        for dc in range(CD):
            sl = dc % 2
            P.wait("sp", "dve", ln_war.get(("x1r", sl), 0))
            if tt == 0 and dc < 2:
                P.wait("sp", "os0", so_x1_all[0])
                P.wait("sp", "os1", so_x1_all[1])
            x_cnt[sl] += 1
            P.op("sp", lambda e_, dc=dc, sl=sl, ts0=ts0: e_.dma_start(
                x1r[sl][:, :], x1_hbm[dc * 128:(dc + 1) * 128, ts0:ts0 + TT]),
                sem="sx%d" % sl, inc=16)
            P.wait("dve", "sx%d" % sl, x_cnt[sl] * 16)
            P.wait("dve", "dve", acc_dve[dc])
            d_r = P.op("dve", lambda v, dc=dc, sl=sl: v.scalar_tensor_tensor(
                acc[:, dc * TT:(dc + 1) * TT], x1r[sl][:, :], 2.0,
                acc[:, dc * TT:(dc + 1) * TT], ALU.mult, ALU.add), sem="dve")
            ln_war[("x1r", sl)] = d_r
            gb2 = dc % 2
            P.wait("act", "dve", d_r)
            P.wait("act", "pe", st_pe[gb2])
            a_sq = P.op("act", lambda s, dc=dc, gb2=gb2: s.activation(
                sqf[gb2][:, :], acc[:, dc * TT:(dc + 1) * TT], AF.Square),
                sem="act")
            P.wait("dve", "pe", st_pe[gb2])
            P.wait("dve", "dve", d_r)
            d_bf = P.op("dve", lambda v, dc=dc, gb2=gb2: v.tensor_copy(
                accbf[gb2][:, :], acc[:, dc * TT:(dc + 1) * TT]), sem="dve")

            def emit_statsf(dc=dc, gb2=gb2, a_sq=a_sq, d_bf=d_bf):
                pmu_ = po[gb2]
                psq_ = pg[gb2]
                for o, w in tile512(TT):
                    P.wait("pe", "act", a_sq)
                    P.wait("pe", "dve", d_bf)
                    P.wait("pe", "dve", row2_dve.get(("r", gb2), 0))
                    P.op("pe", lambda t, o=o, w=w, pmu_=pmu_: t.matmul(
                        pmu_[0:1, o:o + w], ones_col_bf[:, :],
                        accbf[gb2][:, o:o + w], start=True, stop=True),
                        sem=None)
                    p2 = P.op("pe", lambda t, o=o, w=w, psq_=psq_: t.matmul(
                        psq_[0:1, o:o + w], ones_col_bf[:, :],
                        sqf[gb2][:, o:o + w], start=True, stop=True), sem="pe")
                    P.wait("dve", "pe", p2)
                    P.wait("dve", "dve", row2_dve.get("chain", 0))
                    if dc == 0:
                        P.op("dve", lambda v, o=o, w=w, pmu_=pmu_: v.tensor_copy(
                            muf_row[0:1, o:o + w], pmu_[0:1, o:o + w]),
                            sem="dve")
                        d2 = P.op("dve", lambda v, o=o, w=w, psq_=psq_:
                                  v.tensor_copy(
                                      mu2f_row[0:1, o:o + w],
                                      psq_[0:1, o:o + w]), sem="dve")
                    else:
                        P.op("dve", lambda v, o=o, w=w, pmu_=pmu_: v.tensor_add(
                            muf_row[0:1, o:o + w], muf_row[0:1, o:o + w],
                            pmu_[0:1, o:o + w]), sem="dve")
                        d2 = P.op("dve", lambda v, o=o, w=w, psq_=psq_:
                                  v.tensor_add(
                                      mu2f_row[0:1, o:o + w],
                                      mu2f_row[0:1, o:o + w],
                                      psq_[0:1, o:o + w]), sem="dve")
                    row2_dve[("r", gb2)] = d2
                    row2_dve["chain"] = d2
                st_pe[gb2] = P.cnt["pe"]
            pendf.append(emit_statsf)
            if len(pendf) > 1:
                pendf.pop(0)()
        while pendf:
            pendf.pop(0)()
        dr = P.cnt["dve"]
        P.wait("dve", "dve", dr)
        P.op("dve", lambda v: v.tensor_scalar_mul(muf_row[:, :], muf_row[:, :],
                                                  1.0 / D), sem="dve")
        q2 = P.op("dve", lambda v: v.tensor_scalar_mul(
            mu2f_row[:, :], mu2f_row[:, :], 1.0 / D), sem="dve")
        P.wait("dve", "dve", q2)
        q3 = P.op("dve", lambda v: v.tensor_mul(tmpf_row[:, :], muf_row[:, :],
                                                muf_row[:, :]), sem="dve")
        P.wait("dve", "dve", q3)
        q4 = P.op("dve", lambda v: v.tensor_sub(mu2f_row[:, :], mu2f_row[:, :],
                                                tmpf_row[:, :]), sem="dve")
        P.wait("act", "dve", q4)
        q5 = P.op("act", lambda s: s.activation(mu2f_row[:, :], mu2f_row[:, :],
                                                AF.Sqrt, bias=fconst[0:1, 3:4]),
                  sem="act")
        P.wait("dve", "act", q5)
        q6 = P.op("dve", lambda v: v.reciprocal(mu2f_row[:, :], mu2f_row[:, :]),
                  sem="dve")
        dve_fin = P.cnt["dve"]
        P.wait("pe", "dve", dve_fin)
        P.wait("pe", "act", P.cnt["act"])
        for o, w in tile512(TT):
            P.op("pe", lambda t, o=o, w=w: t.matmul(
                pg[0][:, o:o + w], ones_row_f[:, :], muf_row[0:1, o:o + w],
                start=True, stop=True), sem=None)
            sem = "pe" if o + w == TT else None
            P.op("pe", lambda t, o=o, w=w: t.matmul(
                pg[1][:, o:o + w], ones_row_f[:, :], mu2f_row[0:1, o:o + w],
                start=True, stop=True), sem=sem)
        pe_bcf = P.cnt["pe"]
        for dc in range(CD):
            osl = dc % 2
            P.wait("dve", "pe", pe_bcf)
            P.wait("dve", "os%d" % osl, of_war.get(osl, 0))
            d1 = P.op("dve", lambda v, dc=dc, osl=osl: v.tensor_sub(
                of_sb[osl][:, :], acc[:, dc * TT:(dc + 1) * TT], pg[0][:, :]),
                sem="dve")
            P.wait("dve", "dve", d1)
            d2 = P.op("dve", lambda v, osl=osl: v.tensor_mul(
                of_sb[osl][:, :], of_sb[osl][:, :], pg[1][:, :]), sem="dve")
            P.wait("sp", "dve", d2)
            so1 = P.op("sp", lambda e_, dc=dc, osl=osl, ts0=ts0: e_.dma_start(
                out_fm[dc * 128:(dc + 1) * 128, ts0:ts0 + TT], of_sb[osl][:, :]),
                sem="os%d" % osl, inc=16)
            of_war[osl] = so1
        if cfg.safe_barriers in ("both", "tt") and tt == 0:
            P.barrier()
        ln2_dve_end[0] = P.cnt["dve"]
        pg_free[0] = (P.cnt["act"], P.cnt["dve"])
        pg_free[1] = (P.cnt["act"], P.cnt["dve"])
        po_free[0] = P.cnt["dve"]
        po_free[1] = P.cnt["dve"]

    P.wait("sp", "so", P.cnt["so"])
    P.wait("sp", "os0", P.cnt["os0"])
    P.wait("sp", "os1", P.cnt["os1"])
    P.finish()
    return nc, P



def yarn_tables(S):
    DR = 64
    pos2 = np.arange(0, DR, 2, dtype=np.float64)
    inv = 1.0 / (10000.0 ** (pos2 / DR))
    wavelen = 2.0 * np.pi / inv
    ramp = np.clip((8192.0 / wavelen - 1.0) / 3.0, 0.0, 1.0)
    inv = inv * (1.0 - ramp) + inv / 16.0 * ramp
    t = np.arange(S, dtype=np.float64)
    freqs = np.outer(t, inv)
    emb = np.concatenate([freqs, freqs], axis=-1)
    af = 0.1 * math.log(16.0) + 1.0
    cos_t = (np.cos(emb) * af).T.astype(np.float32).copy()
    sin = (np.sin(emb) * af).T.astype(np.float32)
    sin_s = sin.copy()
    sin_s[:32] = -sin_s[:32]
    return cos_t, np.ascontiguousarray(sin_s)


def host_prepare(cfg, inp):
    S, D, CD, QC, KC, H, FC, E, SH, NE = (cfg.S, cfg.D, cfg.CD, cfg.QC, cfg.KC,
                                          cfg.H, cfg.FC, cfg.E, cfg.SH, cfg.NE)
    QL, KL, F = cfg.QL, cfg.KL, cfg.F
    F2, D2 = F // 2, D // 2
    f32 = np.float32
    o = {}
    wqa = np.asarray(inp["L0_attn_w_qa"], f32)
    o["wqa_re"] = np.ascontiguousarray(
        wqa.reshape(CD, 128, 4, QL // 4).transpose(2, 1, 0, 3).reshape(
            4, 128, CD * (QL // 4)))
    o["gq_re"] = np.ascontiguousarray(
        np.asarray(inp["L0_attn_q_norm_w"], f32).reshape(QC, 128).T)
    wqb = np.asarray(inp["L0_attn_w_qb"], f32).reshape(QL, H, 192)
    o["wqb_n_re"] = np.ascontiguousarray(
        wqb[:, :, :128].reshape(QC, 128, H, 128).transpose(2, 1, 0, 3).reshape(
            H, 128, QC * 128))
    r2 = wqb[:, :, 128:].reshape(QC, 128, H // 2, 2, 64).transpose(2, 1, 0, 3, 4)
    o["wqb_r2_re"] = np.ascontiguousarray(r2.reshape(H // 2, 128, QC * 128))
    wkva = np.asarray(inp["L0_attn_w_kva"], f32)
    o["wkva_re"] = np.ascontiguousarray(
        wkva.reshape(CD, 128, KL + 64).transpose(1, 0, 2).reshape(
            128, CD * (KL + 64)))
    o["gkv_re"] = np.ascontiguousarray(
        np.asarray(inp["L0_attn_kv_norm_w"], f32).reshape(KC, 128).T)
    wkvb = np.asarray(inp["L0_attn_w_kvb"], f32).reshape(KL, H, 256)
    o["wkvb_k_re"] = np.ascontiguousarray(
        wkvb[:, :, :128].reshape(KC, 128, H, 128).transpose(2, 1, 0, 3).reshape(
            H, 128, KC * 128))
    o["wkvb_v_re"] = np.ascontiguousarray(
        wkvb[:, :, 128:].reshape(KC, 128, H, 128).transpose(2, 1, 0, 3).reshape(
            H, 128, KC * 128))
    wo = np.asarray(inp["L0_attn_w_o"], f32)
    o["wo_re"] = np.ascontiguousarray(
        wo.reshape(H, 128, CD, 128).transpose(2, 1, 0, 3).reshape(
            CD, 128, H * 128))
    wa = np.asarray(inp["L0_ffn_wa"], f32)
    o["wa_rr"] = np.ascontiguousarray(
        wa.reshape(CD, 128, E).transpose(1, 0, 2).reshape(128, CD * E))
    rgb = np.asarray(inp["L0_ffn_rgb"], f32)
    sgb = np.asarray(inp["L0_ffn_sgb"], f32)
    gb = np.concatenate([rgb.reshape(E, FC, 128), sgb.reshape(SH, FC, 128)], 0)
    o["gb_re"] = np.ascontiguousarray(gb.transpose(2, 0, 1).reshape(128, NE * FC))
    rb1 = np.asarray(inp["L0_ffn_rb1"], f32)
    sb1 = np.asarray(inp["L0_ffn_sb1"], f32)
    b1 = np.concatenate([rb1.reshape(E, FC, 128), sb1.reshape(SH, FC, 128)], 0)
    o["b1_re"] = np.ascontiguousarray(b1.transpose(2, 0, 1).reshape(128, NE * FC))
    o["b2r_re"] = np.ascontiguousarray(np.asarray(inp["L0_ffn_rb2"], f32))
    sb2 = np.asarray(inp["L0_ffn_sb2"], f32)
    o["sb2_cols"] = np.ascontiguousarray(
        sb2.reshape(SH, CD, 128).transpose(2, 0, 1).reshape(128, SH * CD))

    def w_in(a, n):
        return np.ascontiguousarray(
            a.reshape(n, CD, 128, 2, F2).transpose(0, 3, 2, 1, 4).reshape(
                n, 2, 128, CD * F2))

    def w_out(a, n):
        return np.ascontiguousarray(
            a.reshape(n, FC, 128, 2, D2).transpose(0, 3, 2, 1, 4).reshape(
                n, 2, 128, FC * D2))

    o["rg_re"] = w_in(np.asarray(inp["L0_ffn_rg"], f32), E)
    o["rw1_re"] = w_in(np.asarray(inp["L0_ffn_rw1"], f32), E)
    o["rw2_re"] = w_out(np.asarray(inp["L0_ffn_rw2"], f32), E)
    o["sg_re"] = w_in(np.asarray(inp["L0_ffn_sg"], f32), SH)
    o["sw1_re"] = w_in(np.asarray(inp["L0_ffn_sw1"], f32), SH)
    o["sw2_re"] = w_out(np.asarray(inp["L0_ffn_sw2"], f32), SH)
    es = np.zeros((E, E * 128), f32)
    for e in range(E):
        es[e, e * 128:(e + 1) * 128] = 1.0
    o["esel_re"] = es
    cos_t, sin_s = yarn_tables(S)
    o["cos_re"] = cos_t
    o["sin_re"] = sin_s
    m = np.zeros((128, 128), np.float32)
    for i in range(128):
        m[i, :i] = NEG
    o["maskT_re"] = m
    o["ident_re"] = np.eye(128, dtype=f32)
    return o


_CACHE = {}


def kernel(**inputs):
    cfg = Cfg()
    if "full" not in _CACHE:
        _CACHE["full"] = build(cfg)
    nc, P = _CACHE["full"]
    shared = host_prepare(cfg, inputs)
    x = np.asarray(inputs["L0_attn_x"], np.float32)
    in_maps = []
    for c in range(8):
        m = dict(shared)
        m["x_fm"] = np.ascontiguousarray(x[c].T)
        in_maps.append(m)
    res = run_bass_kernel_spmd(nc, in_maps, core_ids=list(range(8)))
    return np.stack([np.asarray(r["out_fm"]).T for r in res.results], axis=0)


# version 12
# speedup vs baseline: 1.1149x; 1.1149x over previous
import math
import numpy as np
import ml_dtypes
import concourse.bass as bass
import concourse.mybir as mybir
from concourse.bass_utils import run_bass_kernel_spmd

F32 = mybir.dt.float32
BF16 = mybir.dt.bfloat16
AF = mybir.ActivationFunctionType
ALU = mybir.AluOpType
AX = mybir.AxisListType

NEG = -1.0e30
ENGS = ("pe", "act", "dve", "pool", "sp")


class Cfg:
    def __init__(self, S=2048, D=2048, QL=1536, KL=512, H=16, F=1024, E=16, SH=2,
                 WL=512, debug=False):
        self.S, self.D, self.QL, self.KL, self.H, self.F, self.E, self.SH = \
            S, D, QL, KL, H, F, E, SH
        self.CD = D // 128
        self.QC = QL // 128
        self.KC = KL // 128
        self.FC = F // 128
        self.NE = E + SH
        self.TT = S // 2
        self.TH = S // 2
        self.WL = WL
        self.NSTRIP = S // WL
        self.MT = S // 128
        self.NT = S // 128
        self.scale = 1.0 / math.sqrt(192.0)
        self.eps_rms = 1e-6
        self.eps_ln = 1e-5
        self.debug = debug
        self.stop_after = None
        self.safe_barriers = None


class Prog:
    def __init__(self, nc, cfg):
        self.nc = nc
        self.cfg = cfg
        self.q = {e: [] for e in ENGS}
        self.sems = {}
        self.cnt = {}
        self.dumps = []

    def sem(self, name):
        self.sems[name] = self.nc.alloc_semaphore(name)
        self.cnt[name] = 0

    def wait(self, eng, name, val):
        if val is None or val <= 0:
            return
        h = self.sems[name]
        self.q[eng].append(lambda e, h=h, v=val: e.wait_ge(h, v))

    def op(self, eng, fn, sem=None, inc=1):
        if sem is None:
            self.q[eng].append(fn)
            return None
        self.cnt[sem] += inc
        h = self.sems[sem]
        self.q[eng].append(lambda e, fn=fn, h=h, inc=inc: fn(e).then_inc(h, inc))
        return self.cnt[sem]

    def barrier(self):
        vals = dict(self.cnt)
        for eng in ENGS:
            for name, v in vals.items():
                self.wait(eng, name, v)

    def dump(self, name, ap, shape, dt=F32):
        if not self.cfg.debug:
            return
        d = self.nc.dram_tensor("dbg_" + name, list(shape), dt,
                                kind="ExternalOutput").ap()
        self.dumps.append(name)
        for s in ("pe", "act", "dve", "pool"):
            self.wait("sp", s, self.cnt[s])
        self.wait("sp", "so", self.cnt["so"])
        self.op("sp", lambda e, d=d, a=ap: e.dma_start(d, a), sem="so", inc=16)
        self.barrier()

    def finish(self):
        with self.nc.Block() as blk:
            @blk.tensor
            def _(t):
                for f in self.q["pe"]:
                    f(t)

            @blk.scalar
            def _(s):
                for f in self.q["act"]:
                    f(s)

            @blk.vector
            def _(v):
                for f in self.q["dve"]:
                    f(v)

            @blk.gpsimd
            def _(g):
                for f in self.q["pool"]:
                    f(g)

            @blk.sync
            def _(s):
                for f in self.q["sp"]:
                    f(s)


def tile512(width):
    o = 0
    while o < width:
        w = min(512, width - o)
        yield o, w
        o += w


def build(cfg: Cfg):
    nc = bass.Bass("TRN2", target_bir_lowering=False)
    S, D, CD, QC, KC, H, FC, E, SH, NE = (cfg.S, cfg.D, cfg.CD, cfg.QC, cfg.KC,
                                          cfg.H, cfg.FC, cfg.E, cfg.SH, cfg.NE)
    QL, KL, F, TH, WL, MT, NT, TT = (cfg.QL, cfg.KL, cfg.F, cfg.TH, cfg.WL,
                                     cfg.MT, cfg.NT, cfg.TT)
    F2, D2 = F // 2, D // 2
    assert CD * F2 == FC * D2

    def din(name, shape):
        return nc.dram_tensor(name, list(shape), F32, kind="ExternalInput").ap()

    x_fm = din("x_fm", [D, S])
    wqa_re = din("wqa_re", [4, 128, CD * (QL // 4)])
    gq_re = din("gq_re", [128, QC])
    wqb_n_re = din("wqb_n_re", [H, 128, QC * 128])
    wqb_r2_re = din("wqb_r2_re", [H // 2, 128, QC * 128])
    wkva_re = din("wkva_re", [128, CD * (KL + 64)])
    gkv_re = din("gkv_re", [128, KC])
    wkvb_k_re = din("wkvb_k_re", [H, 128, KC * 128])
    wkvb_v_re = din("wkvb_v_re", [H, 128, KC * 128])
    wo_re = din("wo_re", [CD, 128, H * 128])
    wa_rr = din("wa_rr", [128, CD * E])
    gb_re = din("gb_re", [128, NE * FC])
    b1_re = din("b1_re", [128, NE * FC])
    b2r_re = din("b2r_re", [E, D])
    sb2_cols = din("sb2_cols", [128, SH * CD])
    rg_re = din("rg_re", [E, 2, 128, CD * F2])
    rw1_re = din("rw1_re", [E, 2, 128, CD * F2])
    rw2_re = din("rw2_re", [E, 2, 128, FC * D2])
    sg_re = din("sg_re", [SH, 2, 128, CD * F2])
    sw1_re = din("sw1_re", [SH, 2, 128, CD * F2])
    sw2_re = din("sw2_re", [SH, 2, 128, FC * D2])
    esel_re = din("esel_re", [E, E * 128])
    cos_re = din("cos_re", [64, S])
    sin_re = din("sin_re", [64, S])
    maskT_re = din("maskT_re", [128, 128])
    ident_re = din("ident_re", [128, 128])

    x1pre_hbm = nc.dram_tensor("x1pre_hbm", [D, S], F32, kind="Internal").ap()
    x1_hbm = nc.dram_tensor("x1_hbm", [D, S], F32, kind="Internal").ap()
    oh_hbm = nc.dram_tensor("oh_hbm", [H, 128, S], BF16, kind="Internal").ap()
    out_fm = nc.dram_tensor("out_fm", [D, S], F32, kind="ExternalOutput").ap()

    P = Prog(nc, cfg)
    for s in ("cst", "xbf", "wq0", "wq1", "wh0", "wh1", "wr0", "wr1",
              "sx", "sx0", "sx1", "so", "os0", "os1", "pe", "act", "dve",
              "pool"):
        P.sem(s)

    def sb(name, shape, dt):
        return nc.alloc_sbuf_tensor(name, list(shape), dt)

    def psum(name, shape):
        return nc.alloc_psum_tensor(name, list(shape), F32)

    cos_t = sb("cos_t", [64, S], BF16)
    sin_t = sb("sin_t", [64, S], BF16)
    maskT = sb("maskT", [128, 128], F32)
    ident = sb("ident", [128, 128], F32)
    gq = sb("gq", [128, QC], F32)
    gkv = sb("gkv", [128, KC], F32)
    wa_sb = sb("wa_sb", [128, CD * E], F32)
    gb_sb = sb("gb_sb", [128, NE * FC], F32)
    b1_sb = sb("b1_sb", [128, NE * FC], F32)
    sb2c = sb("sb2c", [128, SH * CD], F32)
    ssb2 = sb("ssb2", [128, CD], F32)
    c_bf = sb("c_bf", [E, S], BF16)
    ones_col_bf = sb("ones_col_bf", [128, 1], BF16)
    ones_col_f = sb("ones_col_f", [128, 1], F32)
    ones_row_bf = sb("ones_row_bf", [1, 128], BF16)
    ones_row_f = sb("ones_row_f", [1, 128], F32)
    kmax2 = sb("kmax2", [1, 4], F32)
    zrow = sb("zrow", [1, max(E, 16)], F32)
    fconst = sb("fconst", [1, 8], F32)
    sc192 = sb("sc192", [128, 1], F32)
    sb_pers = nc.sbuf_base

    P.op("pool", lambda g: g.dma_start(cos_t[:, :], cos_re), sem="cst", inc=16)
    P.op("pool", lambda g: g.dma_start(sin_t[:, :], sin_re), sem="cst", inc=16)
    for dst, src in ((maskT, maskT_re), (ident, ident_re), (gq, gq_re),
                     (gkv, gkv_re), (wa_sb, wa_rr), (gb_sb, gb_re),
                     (b1_sb, b1_re), (sb2c, sb2_cols)):
        P.op("sp", lambda s, d=dst, r=src: s.dma_start(d[:, :], r), sem="sx",
             inc=16)
    sx_consts = P.cnt["sx"]
    P.op("pool", lambda g: g.memset(ones_col_bf[:, :], 1.0), sem="pool")
    P.op("pool", lambda g: g.memset(ones_col_f[:, :], 1.0), sem="pool")
    P.op("pool", lambda g: g.memset(ones_row_bf[:, :], 1.0), sem="pool")
    P.op("pool", lambda g: g.memset(ones_row_f[:, :], 1.0), sem="pool")
    P.op("pool", lambda g: g.memset(zrow[:, :], 0.0), sem="pool")
    P.op("pool", lambda g: g.memset(fconst[0:1, 0:1], cfg.eps_rms), sem="pool")
    P.op("pool", lambda g: g.memset(fconst[0:1, 1:2], 1.0 / KL), sem="pool")
    P.op("pool", lambda g: g.memset(fconst[0:1, 2:3], 1.0 / QL), sem="pool")
    P.op("pool", lambda g: g.memset(fconst[0:1, 3:4], cfg.eps_ln), sem="pool")
    P.op("pool", lambda g: g.memset(sc192[:, :], cfg.scale), sem="pool")
    pool_init = P.cnt["pool"]
    P.wait("dve", "sx", sx_consts)
    dve_ssb2 = P.op("dve", lambda v: v.tensor_add(
        ssb2[:, :], sb2c[:, 0:CD], sb2c[:, CD:2 * CD]), sem="dve")

    kvn = sb("kvn", [128, KC * S], BF16)
    kra = sb("kra", [128, S], BF16)
    qn = sb("qn", [128, QC * S], BF16)
    sb_markA = nc.sbuf_base

    x_bf = sb("x_bf", [128, CD * S], BF16)
    rstd_row = sb("rstd_row", [1, S], F32)
    sq_a = sb("sq_a", [128, S], BF16)
    rope_tmp = sb("rope_tmp", [128, S], BF16)
    kr_f = sb("kr_f", [64, S], BF16)
    sb_markW = nc.sbuf_base
    wkva_bf = sb("wkva_bf", [128, CD * (KL + 64)], BF16)
    assert nc.sbuf_base <= nc.sbuf_top, f"A1 sbuf over {nc.sbuf_base}"
    nc.sbuf_base = sb_markW
    wqa_bf = [sb("wqa_bf0", [128, CD * (QL // 4)], BF16),
              sb("wqa_bf1", [128, CD * (QL // 4)], BF16)]
    assert nc.sbuf_base <= nc.sbuf_top, f"A2 sbuf over {nc.sbuf_base}"

    P.op("pool", lambda g: g.dma_start(wkva_bf[:, :], wkva_re), sem="cst", inc=16)
    cst_wkva = P.cnt["cst"]
    P.op("pool", lambda g: g.memset(kra[64:65, :], -1.0), sem="pool")
    pool_kra = P.cnt["pool"]
    for dc in range(CD):
        P.op("pool", lambda g, dc=dc: g.dma_start(
            x_bf[:, dc * S:(dc + 1) * S], x_fm[dc * 128:(dc + 1) * 128, :]),
            sem="xbf", inc=16)
    xbf_x = P.cnt["xbf"]

    def early_out():
        P.wait("sp", "so", P.cnt["so"])
        P.wait("sp", "os0", P.cnt["os0"])
        P.wait("sp", "os1", P.cnt["os1"])
        P.finish()

    if cfg.stop_after == "P0":
        early_out()
        return nc, P

    pv = [psum("pv0", [128, TH]), psum("pv1", [128, TH])]
    ps_shared_mark = nc.psum_base
    pst = psum("pst", [1, S])
    nc.psum_base = ps_shared_mark
    pbc = psum("pbc", [128, S])
    assert nc.psum_base <= 8, f"A12 psum over {nc.psum_base}"

    def proj(nchunks, lhs_of, dst_of, chunk_waits, denom, first_waits,
             do_stats=True):
        pv_free = {0: (P.cnt["act"], P.cnt["dve"]),
                   1: (P.cnt["act"], P.cnt["dve"])}
        pend_stats = []
        for c in range(nchunks):
            lhs_fn, cw = lhs_of(c)
            for nm, v in chunk_waits(c):
                P.wait("pe", nm, v)
            for th in range(2):
                bi = (c * 2 + th) % 2
                if bi in pv_free:
                    P.wait("pe", "act", pv_free[bi][0])
                    P.wait("pe", "dve", pv_free[bi][1])
                if c == 0 and th == 0:
                    for nm, v in first_waits:
                        P.wait("pe", nm, v)
                for dc in range(CD):
                    for o, w in tile512(TH):
                        sem = "pe" if (dc == CD - 1 and o + w == TH) else None
                        P.op("pe", lambda t, a=pv[bi][0:cw, o:o + w],
                             b=lhs_fn(dc),
                             r=x_bf[:, dc * S + th * TH + o:
                                    dc * S + th * TH + o + w],
                             s1=(dc == 0), s2=(dc == CD - 1):
                             t.matmul(a, b, r, start=s1, stop=s2), sem=sem)
                pe_mm = P.cnt["pe"]
                P.wait("act", "pe", pe_mm)
                a_sq = P.op("act", lambda s, p=pv[bi][0:cw, 0:TH],
                            d=sq_a[0:cw, th * TH:(th + 1) * TH]:
                            s.activation(d, p, AF.Square), sem="act")
                P.wait("dve", "pe", pe_mm)
                P.wait("dve", "act", a_sq)
                d_cp = P.op("dve", lambda v, p=pv[bi][0:cw, 0:TH],
                            d=dst_of(c, th): v.tensor_copy(d, p), sem="dve")
                if not do_stats:
                    pv_free[bi] = (a_sq, d_cp)
                    continue
                def emit_stats(c=c, th=th, cw=cw, a_sq=a_sq):
                    P.wait("pe", "act", a_sq)
                    for o, w in tile512(TH):
                        first_in_reg = ((th * TH + o) * 4) % 2048 == 0
                        last_in_reg = ((th * TH + o + w) * 4) % 2048 == 0 or \
                            (th == 1 and o + w == TH)
                        sem = "pe" if o + w == TH else None
                        P.op("pe", lambda t, o=o, w=w, cw=cw, th=th, c=c,
                             s1=(c == 0 and first_in_reg),
                             s2=(c == nchunks - 1 and last_in_reg):
                             t.matmul(pst[0:1, th * TH + o:th * TH + o + w],
                                      ones_col_bf[0:cw, :],
                                      sq_a[0:cw, th * TH + o:th * TH + o + w],
                                      start=s1, stop=s2, skip_group_check=True),
                             sem=sem)
                pend_stats.append(emit_stats)
                if len(pend_stats) > 1:
                    pend_stats.pop(0)()
                pv_free[bi] = (a_sq, d_cp)
        while pend_stats:
            pend_stats.pop(0)()
        pe_stats = P.cnt["pe"]
        if not do_stats:
            return 0, 0, pv_free
        P.wait("act", "pe", pe_stats)
        a1 = P.op("act", lambda s, dn=denom: s.activation(
            rstd_row[:, :], pst[0:1, :], AF.Sqrt, bias=fconst[0:1, 0:1],
            scale=fconst[0:1, 1:2] if dn == KL else fconst[0:1, 2:3]),
            sem="act")
        P.wait("dve", "act", a1)
        d2 = P.op("dve", lambda v: v.reciprocal(rstd_row[:, :], rstd_row[:, :]),
                  sem="dve")
        return a1, d2, pv_free

    def bcast_row_f32(row_ap, dst_psum, width, extra_pe_waits):
        for nm, v in extra_pe_waits:
            P.wait("pe", nm, v)
        last = None
        for o, w in tile512(width):
            sem = "pe" if o + w == width else None
            last = P.op("pe", lambda t, o=o, w=w, r=row_ap, d=dst_psum: t.matmul(
                d[:, o:o + w], ones_row_f[:, :], r[0:1, o:o + w],
                start=True, stop=True), sem=sem)
        return P.cnt["pe"]

    first_waits = [("cst", cst_wkva), ("xbf", xbf_x),
                   ("pool", pool_init)]

    def kva_lhs(c):
        return (lambda dc, c=c: wkva_bf[:, dc * (KL + 64) + c * 128:
                                        dc * (KL + 64) + c * 128 + 128], 128)

    stats_evac, dve_kv_rstd, pvfree1 = proj(
        KC, kva_lhs,
        lambda c, th: kvn[:, c * S + th * TH:c * S + th * TH + TH],
        lambda c: [], KL, first_waits,
        do_stats=(cfg.stop_after != "A1c1"))

    if cfg.stop_after in ("A1c1", "A1c2"):
        early_out()
        return nc, P

    act_b4 = P.cnt["act"]
    dve_b4 = P.cnt["dve"]
    for th in range(2):
        bi = th % 2
        P.wait("pe", "act", act_b4)
        P.wait("pe", "dve", dve_b4)
        for dc in range(CD):
            for o, w in tile512(TH):
                sem = "pe" if (dc == CD - 1 and o + w == TH) else None
                P.op("pe", lambda t, bi=bi, dc=dc, th=th, o=o, w=w: t.matmul(
                    pv[bi][0:64, o:o + w],
                    wkva_bf[:, dc * (KL + 64) + KL:dc * (KL + 64) + KL + 64],
                    x_bf[:, dc * S + th * TH + o:dc * S + th * TH + o + w],
                    start=(dc == 0), stop=(dc == CD - 1)), sem=sem)
        pe_mm = P.cnt["pe"]
        P.wait("act", "pe", pe_mm)
        P.op("act", lambda s, bi=bi, th=th: s.activation(
            kr_f[0:64, th * TH:(th + 1) * TH], pv[bi][0:64, 0:TH], AF.Copy),
            sem="act")
    act_krf = P.cnt["act"]

    if cfg.stop_after == "A1c3":
        early_out()
        return nc, P

    pe_bc1 = bcast_row_f32(rstd_row, pbc, S, [("dve", dve_kv_rstd)])
    P.wait("dve", "pe", pe_bc1)
    for c in range(KC):
        d1 = P.op("dve", lambda v, c=c: v.tensor_mul(
            kvn[:, c * S:(c + 1) * S], kvn[:, c * S:(c + 1) * S], pbc[:, :]),
            sem="dve")
        P.wait("dve", "dve", d1)
        P.op("dve", lambda v, c=c: v.tensor_scalar_mul(
            kvn[:, c * S:(c + 1) * S], kvn[:, c * S:(c + 1) * S],
            gkv[:, c:c + 1]), sem="dve")
    dve_kvn = P.cnt["dve"]

    if cfg.stop_after == "A1c4":
        early_out()
        return nc, P

    P.wait("dve", "act", act_krf)
    P.wait("dve", "cst", P.cnt["cst"])
    e1 = P.op("dve", lambda v: v.tensor_copy(rope_tmp[0:32, :], kr_f[32:64, :]),
              sem="dve")
    e2 = P.op("dve", lambda v: v.tensor_copy(rope_tmp[32:64, :], kr_f[0:32, :]),
              sem="dve")
    P.wait("dve", "dve", e2)
    e3 = P.op("dve", lambda v: v.tensor_mul(rope_tmp[0:64, :], rope_tmp[0:64, :],
                                            sin_t[:, :]), sem="dve")
    e4 = P.op("dve", lambda v: v.tensor_mul(kr_f[0:64, :], kr_f[0:64, :],
                                            cos_t[:, :]), sem="dve")
    P.wait("dve", "dve", e4)
    P.wait("dve", "pool", pool_kra)
    P.op("dve", lambda v: v.tensor_add(kra[0:64, :], rope_tmp[0:64, :],
                                       kr_f[0:64, :]), sem="dve")
    dve_kra = P.cnt["dve"]
    P.dump("kvn", kvn[:, :], [128, KC * S], BF16)
    P.dump("kra", kra[0:65, :], [65, S], BF16)

    if cfg.stop_after == "A1":
        early_out()
        return nc, P

    QQ = QC // 4
    wq_cnt = [0, 0]
    pe_a1_end = P.cnt["pe"]
    quarter_end = {}

    def qa_chunk_waits(c):
        qh, j = divmod(c, QQ)
        sl = qh % 2
        if j == 0:
            P.wait("pool", "pe", quarter_end.get(qh - 2, pe_a1_end))
            wq_cnt[sl] += 1
            P.op("pool", lambda g, qh=qh, sl=sl: g.dma_start(
                wqa_bf[sl][:, :], wqa_re[qh]), sem="wq%d" % sl, inc=16)
        if c > 0 and j == 0:
            quarter_end[qh - 1] = P.cnt["pe"]
        return [("wq%d" % sl, wq_cnt[sl] * 16)]

    def qa_lhs(c):
        qh, j = divmod(c, QQ)
        sl = qh % 2
        return (lambda dc, j=j, sl=sl: wqa_bf[sl][:, dc * (QL // 4) + j * 128:
                                                  dc * (QL // 4) + j * 128 + 128],
                128)

    _, dve_q_rstd, pvfree2 = proj(
        QC, qa_lhs,
        lambda c, th: qn[:, c * S + th * TH:c * S + th * TH + TH],
        qa_chunk_waits, QL, [("dve", dve_kvn)])

    pe_bc2 = bcast_row_f32(rstd_row, pbc, S, [("dve", dve_q_rstd)])
    P.wait("dve", "pe", pe_bc2)
    for c in range(QC):
        d1 = P.op("dve", lambda v, c=c: v.tensor_mul(
            qn[:, c * S:(c + 1) * S], qn[:, c * S:(c + 1) * S], pbc[:, :]),
            sem="dve")
        P.wait("dve", "dve", d1)
        P.op("dve", lambda v, c=c: v.tensor_scalar_mul(
            qn[:, c * S:(c + 1) * S], qn[:, c * S:(c + 1) * S], gq[:, c:c + 1]),
            sem="dve")
    P.dump("qn", qn[:, :], [128, QC * S], BF16)

    if cfg.stop_after == "A2":
        early_out()
        return nc, P

    P.barrier()
    nc.sbuf_base = sb_markA
    nc.psum_base = 0

    q_nope = [sb("q_nope0", [128, S], BF16), sb("q_nope1", [128, S], BF16)]
    q_aug = [sb("q_aug0", [128, S], BF16), sb("q_aug1", [128, S], BF16)]
    k_nope = [sb("k_nope0", [128, S], BF16), sb("k_nope1", [128, S], BF16)]
    v_tm = [sb("v_tm0", [128, S], BF16), sb("v_tm1", [128, S], BF16)]
    wkvbk_sb = [sb("wkvbk0", [128, KC * 128], BF16),
                sb("wkvbk1", [128, KC * 128], BF16)]
    wkvbv_sb = [sb("wkvbv0", [128, KC * 128], BF16),
                sb("wkvbv1", [128, KC * 128], BF16)]
    wqbn_sb = [sb("wqbn0", [128, QC * 128], BF16),
               sb("wqbn1", [128, QC * 128], BF16)]
    wqbr_sb = [sb("wqbr0", [128, QC * 128], BF16),
               sb("wqbr1", [128, QC * 128], BF16)]
    qr2_bf = sb("qr2_bf", [128, S], BF16)
    rope_t3 = sb("rope_t3", [128, S], BF16)
    sq_h = sb("sq_h", [128, S], BF16)
    sq_kr = sb("sq_kr", [64, S], BF16)
    sq_qr = sb("sq_qr", [64, S], BF16)
    probsT = [sb("probsT0", [128, WL], BF16), sb("probsT1", [128, WL], BF16)]
    rs_row = sb("rs_row", [1, WL], F32)
    qss_row = sb("qss_row", [1, S], F32)
    rr_bf = sb("rr_bf", [1, WL], BF16)
    rb_sb = sb("rb_sb", [128, WL], F32)
    ostrip = [sb("ostrip0", [128, WL], BF16), sb("ostrip1", [128, WL], BF16)]
    assert nc.sbuf_base <= nc.sbuf_top, f"A3 sbuf over {nc.sbuf_base}"

    psb = [psum("psb0", [128, WL]), psum("psb1", [128, WL])]
    pav = psum("pav", [128, WL])
    prs = psum("prs", [1, WL])
    prb = psum("prb", [128, WL])
    pprep = [psum("pprep0", [128, 512]), psum("pprep1", [128, 512])]
    pkm = psum("pkm", [1, 512])
    assert nc.psum_base <= 8, f"A3 psum over {nc.psum_base}"

    P.wait("act", "dve", P.cnt["dve"])
    act_sqkr = P.op("act", lambda s: s.activation(
        sq_kr[0:64, :], kra[0:64, :], AF.Square), sem="act")

    whd_cnt = [0, 0]
    pkm_dve = [0]
    pkm_act_h = [0]
    qr2_read_dve = [0]
    head_prep_evac = {}
    head_main_done = {}
    prep_psum_free = {}
    probsT_free = {}
    scores_free = {}
    sqh_pe_free = [0]
    kmax_read = [0]
    st_avrs = {"pav": 0, "prs": 0, "prb": 0}
    pend_fin = [None]
    ostrip_free = {}
    ppi = [0]

    def head_loads(h):
        b = h % 2
        P.wait("pool", "pe", P.cnt["pe"])
        P.op("pool", lambda g, h=h, b=b: g.dma_start(
            wkvbk_sb[b][:, :], wkvb_k_re[h]), sem="wh%d" % b, inc=16)
        P.op("pool", lambda g, h=h, b=b: g.dma_start(
            wkvbv_sb[b][:, :], wkvb_v_re[h]), sem="wh%d" % b, inc=16)
        P.op("pool", lambda g, h=h, b=b: g.dma_start(
            wqbn_sb[b][:, :], wqb_n_re[h]), sem="wh%d" % b, inc=16)
        whd_cnt[b] += 3
        if h % 2 == 0:
            P.op("pool", lambda g, h=h, b=b: g.dma_start(
                wqbr_sb[b][:, :], wqb_r2_re[h // 2]), sem="wh%d" % b, inc=16)
            whd_cnt[b] += 1

    def next_pp():
        bi = ppi[0] % 2
        ppi[0] += 1
        fr = prep_psum_free.get(bi)
        if fr is not None:
            P.wait("pe", "act", fr[0])
            P.wait("pe", "dve", fr[1])
        return bi

    def prep(h):
        b = h % 2
        P.wait("pe", "wh%d" % b, whd_cnt[b] * 16)
        pc = head_main_done.get(h - 2)
        if pc is not None:
            P.wait("act", "pe", pc[0])
            P.wait("dve", "pe", pc[0])
        for mt in range(MT):
            bi = next_pp()
            for kc in range(KC):
                sem = "pe" if kc == KC - 1 else None
                P.op("pe", lambda t, bi=bi, kc=kc, mt=mt, b=b: t.matmul(
                    pprep[bi][:, 0:128],
                    kvn[:, kc * S + mt * 128:kc * S + mt * 128 + 128],
                    wkvbv_sb[b][:, kc * 128:(kc + 1) * 128],
                    start=(kc == 0), stop=(kc == KC - 1)), sem=sem)
            pe_mm = P.cnt["pe"]
            P.wait("act", "pe", pe_mm)
            a1 = P.op("act", lambda s, bi=bi, mt=mt, b=b: s.activation(
                v_tm[b][:, mt * 128:(mt + 1) * 128], pprep[bi][:, 0:128],
                AF.Copy), sem="act")
            prep_psum_free[bi] = (a1, P.cnt["dve"])
        if h % 2 == 0:
            for o, w in [(o, min(512, S - o)) for o in range(0, S, 512)]:
                bi = next_pp()
                for qc in range(QC):
                    sem = "pe" if qc == QC - 1 else None
                    P.op("pe", lambda t, bi=bi, qc=qc, o=o, w=w, b=b: t.matmul(
                        pprep[bi][:, 0:w],
                        wqbr_sb[b][:, qc * 128:(qc + 1) * 128],
                        qn[:, qc * S + o:qc * S + o + w],
                        start=(qc == 0), stop=(qc == QC - 1)), sem=sem)
                pe_mm = P.cnt["pe"]
                P.wait("act", "pe", pe_mm)
                P.wait("act", "dve", qr2_read_dve[0])
                a1 = P.op("act", lambda s, bi=bi, o=o, w=w: s.activation(
                    qr2_bf[:, o:o + w], pprep[bi][:, 0:w], AF.Copy), sem="act")
                prep_psum_free[bi] = (a1, P.cnt["dve"])
        rb0 = 0 if h % 2 == 0 else 64
        P.wait("dve", "act", P.cnt["act"])
        r1 = P.op("dve", lambda v, rb0=rb0: v.tensor_copy(
            rope_t3[0:32, :], qr2_bf[rb0 + 32:rb0 + 64, :]), sem="dve")
        r2 = P.op("dve", lambda v, rb0=rb0: v.tensor_copy(
            rope_t3[32:64, :], qr2_bf[rb0:rb0 + 32, :]), sem="dve")
        r2b = P.op("dve", lambda v, rb0=rb0, b=b: v.tensor_copy(
            q_aug[b][0:64, :], qr2_bf[rb0:rb0 + 64, :]), sem="dve")
        P.wait("dve", "dve", r2b)
        P.op("dve", lambda v: v.tensor_mul(rope_t3[0:64, :], rope_t3[0:64, :],
                                           sin_t[:, :]), sem="dve")
        r4 = P.op("dve", lambda v, b=b: v.tensor_mul(
            q_aug[b][0:64, :], q_aug[b][0:64, :], cos_t[:, :]), sem="dve")
        P.wait("dve", "dve", r4)
        r5 = P.op("dve", lambda v, b=b: v.tensor_add(
            q_aug[b][0:64, :], q_aug[b][0:64, :], rope_t3[0:64, :]), sem="dve")
        dve_rope = P.cnt["dve"]
        qr2_read_dve[0] = dve_rope

        ktiles = []
        for o, w in [(o, min(512, S - o)) for o in range(0, S, 512)]:
            bi = next_pp()
            for kc in range(KC):
                sem = "pe" if kc == KC - 1 else None
                P.op("pe", lambda t, bi=bi, kc=kc, o=o, w=w, b=b: t.matmul(
                    pprep[bi][:, 0:w],
                    wkvbk_sb[b][:, kc * 128:(kc + 1) * 128],
                    kvn[:, kc * S + o:kc * S + o + w],
                    start=(kc == 0), stop=(kc == KC - 1)), sem=sem)
            pe_mm = P.cnt["pe"]
            P.wait("act", "pe", pe_mm)
            a1 = P.op("act", lambda s, bi=bi, o=o, w=w, b=b: s.activation(
                k_nope[b][:, o:o + w], pprep[bi][:, 0:w], AF.Copy), sem="act")
            P.wait("act", "act", a1)
            P.wait("act", "pe", sqh_pe_free[0])
            a2 = P.op("act", lambda s, bi=bi, w=w, o=o: s.activation(
                sq_h[:, o:o + w], pprep[bi][:, 0:w], AF.Square), sem="act")
            prep_psum_free[bi] = (a2, P.cnt["dve"])
            ktiles.append((o, w, a2))
        first_o = True
        for o, w, a2 in ktiles:
            P.wait("pe", "act", a2)
            P.wait("pe", "act", act_sqkr)
            P.wait("pe", "act", pkm_act_h[0])
            P.wait("pe", "dve", pkm_dve[0])
            P.op("pe", lambda t, w=w, o=o: t.matmul(
                pkm[0:1, 0:w], ones_col_bf[:, :], sq_h[:, o:o + w],
                start=True, stop=False), sem=None)
            pm = P.op("pe", lambda t, w=w, o=o: t.matmul(
                pkm[0:1, 0:w], ones_col_bf[0:64, :], sq_kr[0:64, o:o + w],
                start=False, stop=True), sem="pe")
            sqh_pe_free[0] = pm
            P.wait("dve", "pe", pm)
            if first_o:
                d1 = P.op("dve", lambda v, w=w: v.reduce_max(
                    kmax2[0:1, 0:1], pkm[0:1, 0:w], axis=AX.X), sem="dve")
            else:
                d1 = P.op("dve", lambda v, w=w: v.reduce_max(
                    kmax2[0:1, 2:3], pkm[0:1, 0:w], axis=AX.X), sem="dve")
                P.wait("dve", "dve", d1)
                P.op("dve", lambda v: v.tensor_max(
                    kmax2[0:1, 0:1], kmax2[0:1, 0:1], kmax2[0:1, 2:3]),
                    sem="dve")
            pkm_dve[0] = P.cnt["dve"]
            first_o = False
        dve_km = P.cnt["dve"]

        qtiles = []
        for o, w in [(o, min(512, S - o)) for o in range(0, S, 512)]:
            bi = next_pp()
            for qc in range(QC):
                sem = "pe" if qc == QC - 1 else None
                P.op("pe", lambda t, bi=bi, qc=qc, o=o, w=w, b=b: t.matmul(
                    pprep[bi][:, 0:w],
                    wqbn_sb[b][:, qc * 128:(qc + 1) * 128],
                    qn[:, qc * S + o:qc * S + o + w],
                    start=(qc == 0), stop=(qc == QC - 1)), sem=sem)
            pe_mm = P.cnt["pe"]
            P.wait("act", "pe", pe_mm)
            a1 = P.op("act", lambda s, bi=bi, o=o, w=w, b=b: s.activation(
                q_nope[b][:, o:o + w], pprep[bi][:, 0:w], AF.Copy), sem="act")
            P.wait("act", "act", a1)
            P.wait("act", "pe", sqh_pe_free[0])
            a2 = P.op("act", lambda s, bi=bi, w=w, o=o: s.activation(
                sq_h[:, o:o + w], pprep[bi][:, 0:w], AF.Square), sem="act")
            prep_psum_free[bi] = (a2, P.cnt["dve"])
            qtiles.append((o, w, a2))
        P.wait("act", "dve", dve_rope)
        a3 = P.op("act", lambda s, b=b: s.activation(
            sq_qr[0:64, :], q_aug[b][0:64, :], AF.Square), sem="act")
        P.wait("pe", "act", pkm_act_h[0])
        last_ev = 0
        for o, w, a2 in qtiles:
            P.wait("pe", "act", a2)
            P.wait("pe", "act", a3)
            P.wait("pe", "dve", pkm_dve[0])
            P.op("pe", lambda t, w=w, o=o: t.matmul(
                pkm[0:1, 0:w], ones_col_bf[:, :], sq_h[:, o:o + w],
                start=True, stop=False), sem=None)
            pm = P.op("pe", lambda t, w=w, o=o: t.matmul(
                pkm[0:1, 0:w], ones_col_bf[0:64, :], sq_qr[0:64, o:o + w],
                start=False, stop=True), sem="pe")
            sqh_pe_free[0] = pm
            P.wait("dve", "pe", pm)
            last_ev = P.op("dve", lambda v, o=o, w=w: v.tensor_copy(
                qss_row[0:1, o:o + w], pkm[0:1, 0:w]), sem="dve")
            pkm_dve[0] = last_ev
        P.wait("act", "dve", last_ev)
        P.wait("act", "dve", dve_km)
        ab = P.op("act", lambda s, b=b: s.activation(
            q_aug[b][64:65, :], qss_row[0:1, :], AF.Sqrt,
            scale=kmax2[0:1, 0:1]), sem="act")
        pkm_act_h[0] = 0
        head_prep_evac[h] = (P.cnt["act"], P.cnt["dve"])

    kmax_act_read = [0]

    def main(h):
        b = h % 2
        P.wait("pe", "act", head_prep_evac[h][0])
        P.wait("pe", "dve", head_prep_evac[h][1])
        for s_ in range(cfg.NSTRIP):
            l0 = s_ * WL
            mt_hi = (s_ + 1) * (WL // 128)

            def emit_scores(mt):
                o = max(0, mt * 128 - l0)
                sbi = mt % 2
                fr = scores_free.get(sbi)
                if fr is not None:
                    P.wait("pe", "act", fr)
                for oo, w in tile512(WL - o):
                    P.op("pe", lambda t, sbi=sbi, mt=mt, o=o, oo=oo, w=w, b=b,
                         l0=l0: t.matmul(
                             psb[sbi][:, o + oo:o + oo + w],
                             k_nope[b][:, mt * 128:(mt + 1) * 128],
                             q_nope[b][:, l0 + o + oo:l0 + o + oo + w],
                             start=True, stop=False), sem=None)
                    sem = "pe" if oo + w == WL - o else None
                    P.op("pe", lambda t, sbi=sbi, mt=mt, o=o, oo=oo, w=w, b=b,
                         l0=l0: t.matmul(
                             psb[sbi][:, o + oo:o + oo + w],
                             kra[0:65, mt * 128:(mt + 1) * 128],
                             q_aug[b][0:65, l0 + o + oo:l0 + o + oo + w],
                             start=False, stop=True), sem=sem)
                pe_sc = P.cnt["pe"]
                dve_mask = None
                if mt * 128 >= l0:
                    P.wait("dve", "pe", pe_sc)
                    dve_mask = P.op("dve", lambda v, sbi=sbi, o=o: v.tensor_add(
                        psb[sbi][:, o:o + 128], psb[sbi][:, o:o + 128],
                        maskT[:, :]), sem="dve")
                P.wait("act", "pe", pe_sc)
                if dve_mask:
                    P.wait("act", "dve", dve_mask)
                pfr = probsT_free.get(sbi)
                if pfr is not None:
                    P.wait("act", "pe", pfr)
                a_exp = P.op("act", lambda s, sbi=sbi, o=o: s.activation(
                    probsT[sbi][:, 0:WL - o], psb[sbi][:, o:WL], AF.Exp,
                    scale=sc192[:, 0:1]), sem="act")
                scores_free[sbi] = a_exp
                return o, sbi, a_exp

            def emit_av(mt, o, sbi, a_exp):
                P.wait("pe", "act", a_exp)
                if mt == 0:
                    P.wait("pe", "dve", st_avrs["pav"])
                    P.wait("pe", "dve", st_avrs["prs"])
                for oo, w in tile512(WL - o):
                    P.op("pe", lambda t, sbi=sbi, mt=mt, o=o, oo=oo, w=w, b=b,
                         hi=mt_hi: t.matmul(
                             pav[:, o + oo:o + oo + w],
                             v_tm[b][:, mt * 128:(mt + 1) * 128],
                             probsT[sbi][:, oo:oo + w],
                             start=(mt == 0), stop=(mt == hi - 1)), sem=None)
                    sem = "pe" if oo + w == WL - o else None
                    P.op("pe", lambda t, sbi=sbi, mt=mt, o=o, oo=oo, w=w,
                         hi=mt_hi: t.matmul(
                             prs[0:1, o + oo:o + oo + w], ones_col_bf[:, :],
                             probsT[sbi][:, oo:oo + w],
                             start=(mt == 0), stop=(mt == hi - 1)), sem=sem)
                probsT_free[sbi] = P.cnt["pe"]

            pend = None
            for mt in range(mt_hi):
                info = emit_scores(mt)
                if mt == min(1, mt_hi - 1) and pend_fin[0] is not None:
                    pend_fin[0]()
                    pend_fin[0] = None
                if pend is not None:
                    emit_av(*pend)
                pend = (mt,) + info
            emit_av(*pend)
            pe_avdone = P.cnt["pe"]
            P.wait("dve", "pe", pe_avdone)
            def _recip(v):
                with nc.allow_low_precision(
                        reason="softmax denom reciprocal consumed as bf16 "
                               "matmul operand (bf16-honest)"):
                    return v.reciprocal(rr_bf[:, :], prs[0:1, :])
            d2 = P.op("dve", _recip, sem="dve")
            st_avrs["prs"] = d2

            def finalize(d2=d2, h=h, l0=l0, s_=s_):
                P.wait("pe", "dve", d2)
                P.wait("pe", "act", st_avrs["prb"])
                for oo, w in tile512(WL):
                    sem = "pe" if oo + w == WL else None
                    P.op("pe", lambda t, oo=oo, w=w: t.matmul(
                        prb[:, oo:oo + w], ones_row_bf[:, :],
                        rr_bf[0:1, oo:oo + w], start=True, stop=True), sem=sem)
                pe_bc = P.cnt["pe"]
                P.wait("act", "pe", pe_bc)
                a2 = P.op("act", lambda s: s.activation(rb_sb[:, :], prb[:, :],
                                                        AF.Copy), sem="act")
                st_avrs["prb"] = a2
                osb = (h * cfg.NSTRIP + s_) % 2
                P.wait("dve", "act", a2)
                ofr = ostrip_free.get(osb)
                if ofr is not None:
                    P.wait("dve", "os%d" % osb, ofr)
                d3 = P.op("dve", lambda v, osb=osb: v.tensor_mul(
                    ostrip[osb][:, :], pav[:, :], rb_sb[:, :]), sem="dve")
                st_avrs["pav"] = d3
                P.wait("sp", "dve", d3)
                so1 = P.op("sp", lambda e, h=h, l0=l0, osb=osb: e.dma_start(
                    oh_hbm[h, :, l0:l0 + WL], ostrip[osb][:, :]),
                    sem="os%d" % osb, inc=16)
                ostrip_free[osb] = so1
            pend_fin[0] = finalize
        if pend_fin[0] is not None:
            pend_fin[0]()
            pend_fin[0] = None
        head_main_done[h] = (P.cnt["pe"], P.cnt["dve"])

    head_loads(0)
    head_loads(1)
    for h in range(H):
        prep(h)
        if h + 2 < H:
            head_loads(h + 2)
        main(h)
        if h == 0:
            P.dump("q_nope0", q_nope[0][:, :], [128, S], BF16)
            P.dump("q_aug0", q_aug[0][0:65, :], [65, S], BF16)
            P.dump("k_nope0", k_nope[0][:, :], [128, S], BF16)
            P.dump("v_tm0", v_tm[0][:, :], [128, S], BF16)

    if cfg.stop_after == "A3":
        early_out()
        return nc, P

    P.barrier()
    nc.sbuf_base = sb_pers
    nc.psum_base = 0

    oa = sb("oa", [128, H * S], BF16)
    wo_sb = [sb("wo_sb0", [128, H * 128], BF16),
             sb("wo_sb1", [128, H * 128], BF16)]
    x_sb = [sb("x_sb0", [128, S], F32), sb("x_sb1", [128, S], F32)]
    x1p = [sb("x1p0", [128, TH], F32), sb("x1p1", [128, TH], F32)]
    x1p_bf = [sb("x1p_bf0", [128, TH], BF16), sb("x1p_bf1", [128, TH], BF16)]
    sq4 = [sb("sq40", [128, TH], BF16), sb("sq41", [128, TH], BF16)]
    mu_row = sb("mu_row", [1, S], F32)
    mu2_row = sb("mu2_row", [1, S], F32)
    rstd1_row = sb("rstd1_row", [1, S], F32)
    nmu_row = sb("nmu_row", [1, S], F32)
    csum_row = sb("csum_row", [1, E], F32)
    colsum_b = sb("colsum_b", [128, E], F32)
    rstd_col = sb("rstd_col", [128, NT], F32)
    nmu_col = sb("nmu_col", [128, NT], F32)
    aff_f = sb("aff_f", [128, NT * E], F32)
    aff_s = sb("aff_s", [128, NT * E], F32)
    m8 = sb("m8", [128, 8], F32)
    s12 = sb("s12", [128, 2], F32)
    cm = sb("cm", [128, E], F32)
    assert nc.sbuf_base <= nc.sbuf_top, f"A4 sbuf over {nc.sbuf_base}"

    pwo = [psum("pwo0", [128, TH]), psum("pwo1", [128, TH])]
    pstm = [psum("pstm0", [1, 512]), psum("pstm1", [1, 512])]
    paff = psum("paff", [128, NT * E + E])
    pmisc = psum("pmisc", [128, 256])
    assert nc.psum_base <= 8, f"A4 psum over {nc.psum_base}"

    P.wait("sp", "os0", P.cnt["os0"])
    P.wait("sp", "os1", P.cnt["os1"])
    for cd in range(H):
        P.op("sp", lambda e, cd=cd: e.dma_start(
            oa[:, cd * S:(cd + 1) * S], oh_hbm[cd]), sem="sx", inc=16)
    sx_oa = P.cnt["sx"]

    aff_started = [False]
    pend4 = []
    wo_free = {}
    wo_cnt2 = [0, 0]
    x_cnt = [0, 0]
    x1p_war = {}
    pwo_free = {}
    row_dve = {}
    sq4_pe = [0, 0]
    x1pbf_pe = [0, 0]

    for dc in range(CD):
        sl = dc % 2
        P.wait("pool", "pe", wo_free.get(sl, 0))
        wo_cnt2[sl] += 1
        whd_cnt[sl] += 1
        P.op("pool", lambda g, dc=dc, sl=sl: g.dma_start(
            wo_sb[sl][:, :], wo_re[dc]), sem="wh%d" % sl, inc=16)
        P.wait("sp", "dve", x1p_war.get(("x", sl), 0))
        x_cnt[sl] += 1
        P.op("sp", lambda e, dc=dc, sl=sl: e.dma_start(
            x_sb[sl][:, :], x_fm[dc * 128:(dc + 1) * 128, :]),
            sem="sx%d" % sl, inc=16)
        for th in range(2):
            ts0 = th * TH
            bi = (dc * 2 + th) % 2
            P.wait("pe", "wh%d" % sl, whd_cnt[sl] * 16)
            P.wait("pe", "sx", sx_oa)
            if bi in pwo_free:
                P.wait("pe", "dve", pwo_free[bi])
            for cd in range(H):
                for o, w in tile512(TH):
                    sem = "pe" if (cd == H - 1 and o + w == TH) else None
                    P.op("pe", lambda t, bi=bi, cd=cd, o=o, w=w, sl=sl, ts0=ts0:
                         t.matmul(pwo[bi][:, o:o + w],
                                  wo_sb[sl][:, cd * 128:(cd + 1) * 128],
                                  oa[:, cd * S + ts0 + o:cd * S + ts0 + o + w],
                                  start=(cd == 0), stop=(cd == H - 1)), sem=sem)
            pe_wo = P.cnt["pe"]
            wo_free[sl] = pe_wo
            P.wait("dve", "sx%d" % sl, x_cnt[sl] * 16)
            P.wait("dve", "pe", pe_wo)
            P.wait("dve", "os%d" % bi, x1p_war.get(("st", bi), 0))
            P.wait("dve", "pe", x1p_war.get(("aff", bi), 0))
            d_res = P.op("dve", lambda v, bi=bi, sl=sl, ts0=ts0: v.tensor_add(
                x1p[bi][:, :], pwo[bi][:, :], x_sb[sl][:, ts0:ts0 + TH]),
                sem="dve")
            pwo_free[bi] = d_res
            x1p_war[("x", sl)] = d_res
            P.wait("sp", "dve", d_res)
            so_x1p = P.op("sp", lambda e, dc=dc, ts0=ts0, bi=bi: e.dma_start(
                x1pre_hbm[dc * 128:(dc + 1) * 128, ts0:ts0 + TH], x1p[bi][:, :]),
                sem="os%d" % bi, inc=16)
            x1p_war[("st", bi)] = so_x1p
            gb_ = (dc * 2 + th) % 2
            P.wait("act", "dve", d_res)
            P.wait("act", "pe", sq4_pe[gb_])
            a_sq = P.op("act", lambda s, bi=bi, gb_=gb_: s.activation(
                sq4[gb_][:, :], x1p[bi][:, :], AF.Square), sem="act")
            P.wait("dve", "pe", x1pbf_pe[gb_])
            P.wait("dve", "dve", d_res)
            d_bf = P.op("dve", lambda v, bi=bi, gb_=gb_: v.tensor_copy(
                x1p_bf[gb_][:, :], x1p[bi][:, :]), sem="dve")

            def emit_stats4(dc=dc, th=th, ts0=ts0, gb_=gb_, a_sq=a_sq,
                            d_bf=d_bf):
                for o, w in tile512(TH):
                    P.wait("pe", "act", a_sq)
                    P.wait("pe", "dve", d_bf)
                    P.wait("pe", "dve", row_dve.get("pstm", 0))
                    P.op("pe", lambda t, o=o, w=w: t.matmul(
                        pstm[0][0:1, 0:w], ones_col_bf[:, :],
                        x1p_bf[gb_][:, o:o + w], start=True, stop=True),
                        sem=None)
                    p2 = P.op("pe", lambda t, o=o, w=w: t.matmul(
                        pstm[1][0:1, 0:w], ones_col_bf[:, :],
                        sq4[gb_][:, o:o + w], start=True, stop=True), sem="pe")
                    P.wait("dve", "pe", p2)
                    if dc == 0:
                        P.op("dve", lambda v, o=o, w=w, ts0=ts0: v.tensor_copy(
                            mu_row[0:1, ts0 + o:ts0 + o + w],
                            pstm[0][0:1, 0:w]), sem="dve")
                        d2 = P.op("dve", lambda v, o=o, w=w, ts0=ts0:
                                  v.tensor_copy(
                                      mu2_row[0:1, ts0 + o:ts0 + o + w],
                                      pstm[1][0:1, 0:w]), sem="dve")
                    else:
                        P.op("dve", lambda v, o=o, w=w, ts0=ts0: v.tensor_add(
                            mu_row[0:1, ts0 + o:ts0 + o + w],
                            mu_row[0:1, ts0 + o:ts0 + o + w],
                            pstm[0][0:1, 0:w]), sem="dve")
                        d2 = P.op("dve", lambda v, o=o, w=w, ts0=ts0:
                                  v.tensor_add(
                                      mu2_row[0:1, ts0 + o:ts0 + o + w],
                                      mu2_row[0:1, ts0 + o:ts0 + o + w],
                                      pstm[1][0:1, 0:w]), sem="dve")
                    row_dve["pstm"] = d2
                sq4_pe[gb_] = P.cnt["pe"]
                x1pbf_pe[gb_] = P.cnt["pe"]
            pend4.append(emit_stats4)
            if len(pend4) > 1:
                pend4.pop(0)()
            P.wait("pe", "dve", d_res)
            for tti in range(TH // 128):
                tglob = th * (TH // 128) + tti
                st = not aff_started[0]
                aff_started[0] = True
                P.op("pe", lambda t, tti=tti, tglob=tglob, bi=bi, dc=dc, st=st:
                     t.matmul(paff[:, tglob * E:(tglob + 1) * E],
                              x1p[bi][:, tti * 128:(tti + 1) * 128],
                              wa_sb[:, dc * E:(dc + 1) * E],
                              start=st, stop=False, skip_group_check=True),
                     sem=None)
            if th == 0:
                P.op("pe", lambda t, dc=dc: t.matmul(
                    paff[0:1, NT * E:NT * E + E], ones_col_f[:, :],
                    wa_sb[:, dc * E:(dc + 1) * E],
                    start=False, stop=False, skip_group_check=True), sem=None)
            x1p_war[("aff", bi)] = P.cnt["pe"]
    while pend4:
        pend4.pop(0)()
    P.op("pe", lambda t: t.matmul(
        paff[0:1, NT * E:NT * E + E], ones_col_f[0:1, :], zrow[0:1, 0:E],
        start=False, stop=True, skip_group_check=True), sem="pe")
    pe_aff_all = P.cnt["pe"]

    P.wait("dve", "pe", pe_aff_all)
    P.wait("dve", "dve", P.cnt["dve"])
    f1 = P.op("dve", lambda v: v.tensor_scalar_mul(mu_row[:, :], mu_row[:, :],
                                                   1.0 / D), sem="dve")
    f2 = P.op("dve", lambda v: v.tensor_scalar_mul(mu2_row[:, :], mu2_row[:, :],
                                                   1.0 / D), sem="dve")
    P.wait("dve", "dve", f2)
    f3 = P.op("dve", lambda v: v.tensor_mul(rstd1_row[:, :], mu_row[:, :],
                                            mu_row[:, :]), sem="dve")
    P.wait("dve", "dve", f3)
    f4 = P.op("dve", lambda v: v.tensor_sub(mu2_row[:, :], mu2_row[:, :],
                                            rstd1_row[:, :]), sem="dve")
    P.wait("act", "dve", f4)
    f5 = P.op("act", lambda s: s.activation(rstd1_row[:, :], mu2_row[:, :],
                                            AF.Sqrt, bias=fconst[0:1, 3:4]),
              sem="act")
    P.wait("dve", "act", f5)
    f6 = P.op("dve", lambda v: v.reciprocal(rstd1_row[:, :], rstd1_row[:, :]),
              sem="dve")
    P.wait("dve", "dve", f6)
    f7 = P.op("dve", lambda v: v.tensor_mul(nmu_row[:, :], mu_row[:, :],
                                            rstd1_row[:, :]), sem="dve")
    P.wait("dve", "dve", f7)
    f8 = P.op("dve", lambda v: v.tensor_scalar_mul(nmu_row[:, :], nmu_row[:, :],
                                                   -1.0), sem="dve")
    dve_rows = P.cnt["dve"]

    P.wait("pe", "dve", dve_rows)
    P.wait("pe", "sx", sx_consts)
    for tti in range(NT):
        P.op("pe", lambda t, tti=tti: t.transpose(
            pmisc[:, 0:1], rstd1_row[0:1, tti * 128:(tti + 1) * 128],
            ident[0:1, 0:1]), sem=None)
        p2 = P.op("pe", lambda t, tti=tti: t.transpose(
            pmisc[:, 1:2], nmu_row[0:1, tti * 128:(tti + 1) * 128],
            ident[0:1, 0:1]), sem="pe")
        P.wait("dve", "pe", p2)
        P.op("dve", lambda v, tti=tti: v.tensor_copy(
            rstd_col[:, tti:tti + 1], pmisc[:, 0:1]), sem="dve")
        d2 = P.op("dve", lambda v, tti=tti: v.tensor_copy(
            nmu_col[:, tti:tti + 1], pmisc[:, 1:2]), sem="dve")
        P.wait("pe", "dve", d2)
    dve_cols = P.cnt["dve"]

    P.wait("dve", "pe", pe_aff_all)
    dcs = P.op("dve", lambda v: v.tensor_copy(
        csum_row[:, :], paff[0:1, NT * E:NT * E + E]), sem="dve")
    P.wait("pe", "dve", dcs)
    pc1 = P.op("pe", lambda t: t.matmul(pmisc[:, 16:16 + E], ones_row_f[:, :],
                                        csum_row[0:1, :], start=True, stop=True),
               sem="pe")
    P.wait("dve", "pe", pc1)
    dcb = P.op("dve", lambda v: v.tensor_copy(colsum_b[:, :],
                                              pmisc[:, 16:16 + E]), sem="dve")

    P.wait("dve", "dve", dve_cols)
    P.wait("dve", "dve", dcb)
    for tti in range(NT):
        t1 = P.op("dve", lambda v, tti=tti: v.tensor_scalar_mul(
            aff_f[:, tti * E:(tti + 1) * E], paff[:, tti * E:(tti + 1) * E],
            rstd_col[:, tti:tti + 1]), sem="dve")
        P.wait("dve", "dve", t1)
        P.op("dve", lambda v, tti=tti: v.scalar_tensor_tensor(
            aff_f[:, tti * E:(tti + 1) * E], colsum_b[:, :],
            nmu_col[:, tti:tti + 1], aff_f[:, tti * E:(tti + 1) * E],
            ALU.mult, ALU.add), sem="dve")
    dve_afff = P.cnt["dve"]
    P.wait("act", "dve", dve_afff)
    a_sig = P.op("act", lambda s: s.activation(aff_s[:, :], aff_f[:, :],
                                               AF.Sigmoid), sem="act")
    P.wait("dve", "act", a_sig)
    for tti in range(NT):
        g1 = P.op("dve", lambda v, tti=tti: v.max(
            out=m8[:, :], in_=aff_s[:, tti * E:(tti + 1) * E]), sem="dve")
        P.wait("dve", "dve", g1)
        g2 = P.op("dve", lambda v: v.tensor_add(s12[:, 0:1], m8[:, 0:1],
                                                m8[:, 1:2]), sem="dve")
        P.wait("dve", "dve", g2)
        g3 = P.op("dve", lambda v: v.reciprocal(s12[:, 1:2], s12[:, 0:1]),
                  sem="dve")
        g4 = P.op("dve", lambda v, tti=tti: v.tensor_scalar(
            cm[:, :], aff_s[:, tti * E:(tti + 1) * E], m8[:, 1:2], None,
            ALU.is_ge), sem="dve")
        P.wait("dve", "dve", g4)
        g5 = P.op("dve", lambda v, tti=tti: v.tensor_mul(
            cm[:, :], cm[:, :], aff_s[:, tti * E:(tti + 1) * E]), sem="dve")
        P.wait("dve", "dve", g5)
        g6 = P.op("dve", lambda v: v.tensor_scalar_mul(cm[:, :], cm[:, :],
                                                       s12[:, 1:2]), sem="dve")
        P.wait("pe", "dve", g6)
        p1 = P.op("pe", lambda t: t.transpose(pmisc[0:E, 32:32 + 128], cm[:, :],
                                              ident[:, :]), sem="pe")
        P.wait("dve", "pe", p1)
        g7 = P.op("dve", lambda v, tti=tti: v.tensor_copy(
            c_bf[:, tti * 128:(tti + 1) * 128], pmisc[0:E, 32:32 + 128]),
            sem="dve")
        P.wait("pe", "dve", g7)
    P.dump("aff_s", aff_s[:, :], [128, NT * E])
    P.dump("c_bf", c_bf[:, :], [E, S], BF16)
    P.dump("mu_row", mu_row[:, :], [1, S])
    P.dump("rstd1_row", rstd1_row[:, :], [1, S])

    if cfg.stop_after == "A4":
        early_out()
        return nc, P

    a4_end = dict(P.cnt)
    P.barrier()
    nc.psum_base = 0
    pmu = psum("pmu", [128, S])
    prst = psum("prst", [128, S])
    assert nc.psum_base <= 8
    sb_a4b = nc.sbuf_base
    x1n = [sb("x1n0", [128, S], F32), sb("x1n1", [128, S], F32)]
    x1t = sb("x1t", [128, S], F32)
    assert nc.sbuf_base <= nc.sbuf_top

    bcast_row_f32(mu_row, pmu, S, [])
    pe_bc3 = bcast_row_f32(rstd1_row, prst, S, [])
    x1n_war = {}
    x1_store_half = [(0, 0), (0, 0)]
    x1t_war = [0]
    for dc in range(CD):
        sl = dc % 2
        P.wait("sp", "dve", x1n_war.get(("ld", sl), 0))
        P.wait("sp", "os%d" % sl, x1n_war.get(("st", sl), 0))
        x_cnt[sl] += 1
        P.op("sp", lambda e, dc=dc, sl=sl: e.dma_start(
            x1n[sl][:, :], x1pre_hbm[dc * 128:(dc + 1) * 128, :]),
            sem="sx%d" % sl, inc=16)
        P.wait("dve", "sx%d" % sl, x_cnt[sl] * 16)
        P.wait("dve", "pe", pe_bc3)
        P.wait("dve", "dve", x1t_war[0])
        d1 = P.op("dve", lambda v, sl=sl: v.tensor_sub(
            x1t[:, :], x1n[sl][:, :], pmu[:, :]), sem="dve")
        P.wait("dve", "dve", d1)
        P.wait("dve", "os%d" % sl, x1n_war.get(("st", sl), 0))
        d2 = P.op("dve", lambda v, sl=sl: v.tensor_mul(
            x1n[sl][:, :], x1t[:, :], prst[:, :]), sem="dve")
        x1n_war[("ld", sl)] = d2
        x1t_war[0] = d2
        P.wait("sp", "dve", d2)
        so1 = P.op("sp", lambda e, dc=dc, sl=sl: e.dma_start(
            x1_hbm[dc * 128:(dc + 1) * 128, :], x1n[sl][:, :]),
            sem="os%d" % sl, inc=16)
        x1n_war[("st", sl)] = so1
        if dc == CD // 2 - 1:
            x1_store_half[0] = (P.cnt["os0"], P.cnt["os1"])
    x1_store_half[1] = (P.cnt["os0"], P.cnt["os1"])
    so_x1_all = (P.cnt["os0"], P.cnt["os1"])
    P.dump("x1", x1n[(CD - 1) % 2][:, :], [128, S])

    if cfg.stop_after == "A4b":
        early_out()
        return nc, P

    if cfg.safe_barriers in ("both", "f"):
        P.barrier()
    dve_a4b_end = P.cnt["dve"]
    nc.sbuf_base = sb_pers
    nc.psum_base = 0
    b2r_bf = sb("b2r_bf", [E, D], BF16)
    esel = sb("esel", [E, E * 128], BF16)
    if cfg.safe_barriers == "f2":
        for nm_ in ("pe", "act", "dve", "os0", "os1", "sx", "sx0", "sx1",
                    "wh0", "wh1", "cst", "xbf", "so", "pool"):
            P.wait("pool", nm_, P.cnt[nm_])
    else:
        P.wait("pool", "pe", a4_end["pe"])
        P.wait("pool", "act", a4_end["act"])
        P.wait("pool", "dve", a4_end["dve"])
    P.op("pool", lambda g: g.dma_start(b2r_bf[:, :], b2r_re), sem="cst", inc=16)
    P.op("pool", lambda g: g.dma_start(esel[:, :], esel_re), sem="cst", inc=16)
    cst_f = P.cnt["cst"]
    x1b = sb("x1b", [128, CD * TT], BF16)
    wr_sb = [sb("wr_sb0", [128, CD * F2], BF16),
             sb("wr_sb1", [128, CD * F2], BF16)]
    g_act = sb("g_act", [128, FC * TT], BF16)
    acc = sb("acc", [128, CD * TT], F32)
    c_bc = [sb("c_bc0", [128, TT], BF16)]
    x1r = [sb("x1r0", [128, TT], F32), sb("x1r1", [128, TT], F32)]
    sqf = [sb("sqf0", [128, TT], BF16), sb("sqf1", [128, TT], BF16)]
    accbf = [sb("accbf0", [128, TT], BF16), sb("accbf1", [128, TT], BF16)]
    muf_row = sb("muf_row", [1, TT], F32)
    mu2f_row = sb("mu2f_row", [1, TT], F32)
    tmpf_row = sb("tmpf_row", [1, TT], F32)
    of_sb = [sb("of_sb0", [128, TT], F32), sb("of_sb1", [128, TT], F32)]
    assert nc.sbuf_base <= nc.sbuf_top, f"F sbuf over {nc.sbuf_base}"

    pg = [psum("pg0", [128, TT]), psum("pg1", [128, TT])]
    po = [psum("po0", [128, TT]), psum("po1", [128, TT])]
    assert nc.psum_base <= 8

    wr_cnt = [0, 0]
    wr_free = {}
    ln_war = {}
    of_war = {}
    row2_dve = {}
    ln2_dve_end = [0]
    gact_w2_pe = [0]
    gact_g_act = {}
    gact_y_dve = {}
    pg_free = {}
    po_free = {}
    cbc_free = {}
    x1b_pe_done = [0]

    def slot_src(e, k):
        routed = e < E
        if k < 2:
            return rg_re[e, k] if routed else sg_re[e - E, k]
        if k < 4:
            return rw1_re[e, k - 2] if routed else sw1_re[e - E, k - 2]
        return rw2_re[e, k - 4] if routed else sw2_re[e - E, k - 4]

    def load_slot(e, k, gate_pe):
        sl = (e * 6 + k) % 2
        P.wait("pool", "pe", max(gate_pe, wr_free.get(sl, 0)))
        wr_cnt[sl] += 1
        P.op("pool", lambda g, sl=sl, src=slot_src(e, k): g.dma_start(
            wr_sb[sl][:, :], src), sem="wr%d" % sl, inc=16)
        return sl, wr_cnt[sl]

    for tt in range(2):
        ts0 = tt * TT
        acc_dve = {}
        phase_end = {}
        slot_q = {}
        slot_q[(0, 0)] = load_slot(0, 0, 0)
        slot_q[(0, 1)] = load_slot(0, 1, 0)
        if tt == 1:
            P.wait("pool", "pe", x1b_pe_done[0])
        xbf_half = [0, 0]
        for half in range(2):
            if tt == 0:
                P.wait("pool", "os0", x1_store_half[half][0])
                P.wait("pool", "os1", x1_store_half[half][1])
            for dc in range(half * (CD // 2), (half + 1) * (CD // 2)):
                P.op("pool", lambda g, dc=dc, ts0=ts0: g.dma_start(
                    x1b[:, dc * TT:(dc + 1) * TT],
                    x1_hbm[dc * 128:(dc + 1) * 128, ts0:ts0 + TT]),
                    sem="wq%d" % half, inc=16)
            xbf_half[half] = P.cnt["wq%d" % half]
        if tt == 1:
            P.wait("dve", "act", P.cnt["act"])
        for e in range(NE):
            routed = e < E
            cbi = 0
            if routed:
                P.wait("pe", "cst", cst_f)
                if e == 0 and tt == 0:
                    P.wait("pe", "dve", dve_a4b_end)
                fr = pg_free.get(0)
                if fr is not None:
                    P.wait("pe", "act", fr[0])
                    P.wait("pe", "dve", fr[1])
                for o, w in tile512(TT):
                    sem = "pe" if o + w == TT else None
                    P.op("pe", lambda t, e=e, o=o, w=w, ts0=ts0: t.matmul(
                        pg[0][:, o:o + w], esel[:, e * 128:(e + 1) * 128],
                        c_bf[:, ts0 + o:ts0 + o + w], start=True, stop=True),
                        sem=sem)
                pe_cbc = P.cnt["pe"]
                P.wait("act", "pe", pe_cbc)
                P.wait("act", "dve", cbc_free.get(cbi, 0))
                if e == 0 and tt == 0:
                    P.wait("act", "os0", so_x1_all[0])
                    P.wait("act", "os1", so_x1_all[1])
                a_cbc = P.op("act", lambda s, cbi=cbi: s.activation(
                    c_bc[cbi][:, :], pg[0][:, :], AF.Copy), sem="act")
                pg_free[0] = (a_cbc, pg_free.get(0, (0, 0))[1])
            else:
                a_cbc = None
            for k in range(6):
                if (e, k) not in slot_q:
                    slot_q[(e, k)] = load_slot(e, k, phase_end.get((e, k - 2), 0)
                                               if k >= 2 else
                                               phase_end.get((e - 1, k + 4), 0))
                sl, cnt = slot_q[(e, k)]
                P.wait("pe", "wr%d" % sl, cnt * 16)
                if k < 4:
                    half = k % 2
                    isG = k < 2
                    for jj in range(FC // 2):
                        fc = half * (FC // 2) + jj
                        bi = fc % 2
                        fr = pg_free.get(bi)
                        if fr is not None:
                            P.wait("pe", "act", fr[0])
                            P.wait("pe", "dve", fr[1])
                        for dcc in range(CD):
                            if dcc == 0:
                                P.wait("pe", "wq0", xbf_half[0])
                            if dcc == CD // 2:
                                P.wait("pe", "wq1", xbf_half[1])
                            for o, w in tile512(TT):
                                sem = "pe" if (dcc == CD - 1 and o + w == TT) \
                                    else None
                                P.op("pe", lambda t, bi=bi, sl=sl, dcc=dcc,
                                     jj=jj, o=o, w=w: t.matmul(
                                         pg[bi][:, o:o + w],
                                         wr_sb[sl][:, dcc * F2 + jj * 128:
                                                   dcc * F2 + jj * 128 + 128],
                                         x1b[:, dcc * TT + o:dcc * TT + o + w],
                                         start=(dcc == 0), stop=(dcc == CD - 1)),
                                     sem=sem)
                        pe_p = P.cnt["pe"]
                        if isG:
                            P.wait("act", "pe", pe_p)
                            P.wait("act", "pe", gact_w2_pe[0])
                            a_g = P.op("act", lambda s, bi=bi, e=e, fc=fc:
                                       s.activation(
                                           g_act[:, fc * TT:(fc + 1) * TT],
                                           pg[bi][:, :], AF.Gelu,
                                           bias=gb_sb[:, e * FC + fc:
                                                      e * FC + fc + 1]),
                                       sem="act")
                            pg_free[bi] = (a_g, pg_free.get(bi, (0, 0))[1])
                            gact_g_act[fc] = a_g
                        else:
                            P.wait("dve", "pe", pe_p)
                            P.wait("dve", "act", gact_g_act[fc])
                            d_y = P.op("dve", lambda v, bi=bi, e=e, fc=fc:
                                       v.scalar_tensor_tensor(
                                           g_act[:, fc * TT:(fc + 1) * TT],
                                           pg[bi][:, :],
                                           b1_sb[:, e * FC + fc:
                                                 e * FC + fc + 1],
                                           g_act[:, fc * TT:(fc + 1) * TT],
                                           ALU.add, ALU.mult), sem="dve")
                            if routed:
                                P.wait("dve", "dve", d_y)
                                P.wait("dve", "act", a_cbc)
                                d_y = P.op("dve", lambda v, fc=fc, cbi=cbi:
                                           v.tensor_mul(
                                               g_act[:, fc * TT:(fc + 1) * TT],
                                               g_act[:, fc * TT:(fc + 1) * TT],
                                               c_bc[cbi][:, :]), sem="dve")
                            pg_free[bi] = (pg_free.get(bi, (0, 0))[0], d_y)
                            gact_y_dve[fc] = d_y
                else:
                    half = k - 4
                    for jj in range(CD // 2):
                        dco = half * (CD // 2) + jj
                        bi = dco % 2
                        P.wait("pe", "dve", po_free.get(bi, 0))
                        seeded = False
                        if e == 0:
                            for o, w in tile512(TT):
                                P.op("pe", lambda t, bi=bi, dco=dco, o=o, w=w,
                                     ts0=ts0: t.matmul(
                                         po[bi][:, o:o + w],
                                         b2r_bf[:, dco * 128:dco * 128 + 128],
                                         c_bf[:, ts0 + o:ts0 + o + w],
                                         start=True, stop=False), sem=None)
                            seeded = True
                        for fc in range(FC):
                            if jj == 0:
                                P.wait("pe", "dve", gact_y_dve[fc])
                            for o, w in tile512(TT):
                                sem = "pe" if (fc == FC - 1 and o + w == TT) \
                                    else None
                                st = (fc == 0) and not seeded
                                P.op("pe", lambda t, bi=bi, sl=sl, fc=fc, jj=jj,
                                     o=o, w=w, st=st: t.matmul(
                                         po[bi][:, o:o + w],
                                         wr_sb[sl][:, fc * D2 + jj * 128:
                                                   fc * D2 + jj * 128 + 128],
                                         g_act[:, fc * TT + o:fc * TT + o + w],
                                         start=st, stop=(fc == FC - 1)),
                                     sem=sem)
                        pe_w2 = P.cnt["pe"]
                        gact_w2_pe[0] = pe_w2
                        P.wait("dve", "pe", pe_w2)
                        if e == 0:
                            P.wait("dve", "dve", ln2_dve_end[0])
                            d_a = P.op("dve", lambda v, bi=bi, dco=dco:
                                       v.tensor_scalar_add(
                                           acc[:, dco * TT:(dco + 1) * TT],
                                           po[bi][:, :], ssb2[:, dco:dco + 1]),
                                       sem="dve")
                        else:
                            P.wait("dve", "dve", acc_dve.get(dco, 0))
                            d_a = P.op("dve", lambda v, bi=bi, dco=dco:
                                       v.tensor_add(
                                           acc[:, dco * TT:(dco + 1) * TT],
                                           acc[:, dco * TT:(dco + 1) * TT],
                                           po[bi][:, :]), sem="dve")
                        acc_dve[dco] = d_a
                        po_free[bi] = d_a
                wr_free[sl] = P.cnt["pe"]
                phase_end[(e, k)] = P.cnt["pe"]
                if k == 5 and e + 1 < NE:
                    slot_q[(e + 1, 0)] = load_slot(e + 1, 0, phase_end[(e, 4)])
                    slot_q[(e + 1, 1)] = load_slot(e + 1, 1, phase_end[(e, 5)])
            x1b_pe_done[0] = P.cnt["pe"]
            if routed:
                cbc_free[cbi] = gact_y_dve[FC - 1]
        st_pe = [0, 0]
        pendf = []
        for dc in range(CD):
            sl = dc % 2
            P.wait("sp", "dve", ln_war.get(("x1r", sl), 0))
            if tt == 0 and dc < 2:
                P.wait("sp", "os0", so_x1_all[0])
                P.wait("sp", "os1", so_x1_all[1])
            x_cnt[sl] += 1
            P.op("sp", lambda e_, dc=dc, sl=sl, ts0=ts0: e_.dma_start(
                x1r[sl][:, :], x1_hbm[dc * 128:(dc + 1) * 128, ts0:ts0 + TT]),
                sem="sx%d" % sl, inc=16)
            P.wait("dve", "sx%d" % sl, x_cnt[sl] * 16)
            P.wait("dve", "dve", acc_dve[dc])
            d_r = P.op("dve", lambda v, dc=dc, sl=sl: v.scalar_tensor_tensor(
                acc[:, dc * TT:(dc + 1) * TT], x1r[sl][:, :], 2.0,
                acc[:, dc * TT:(dc + 1) * TT], ALU.mult, ALU.add), sem="dve")
            ln_war[("x1r", sl)] = d_r
            gb2 = dc % 2
            P.wait("act", "dve", d_r)
            P.wait("act", "pe", st_pe[gb2])
            a_sq = P.op("act", lambda s, dc=dc, gb2=gb2: s.activation(
                sqf[gb2][:, :], acc[:, dc * TT:(dc + 1) * TT], AF.Square),
                sem="act")
            P.wait("dve", "pe", st_pe[gb2])
            P.wait("dve", "dve", d_r)
            d_bf = P.op("dve", lambda v, dc=dc, gb2=gb2: v.tensor_copy(
                accbf[gb2][:, :], acc[:, dc * TT:(dc + 1) * TT]), sem="dve")

            def emit_statsf(dc=dc, gb2=gb2, a_sq=a_sq, d_bf=d_bf):
                pmu_ = po[gb2]
                psq_ = pg[gb2]
                for o, w in tile512(TT):
                    P.wait("pe", "act", a_sq)
                    P.wait("pe", "dve", d_bf)
                    P.wait("pe", "dve", row2_dve.get(("r", gb2), 0))
                    P.op("pe", lambda t, o=o, w=w, pmu_=pmu_: t.matmul(
                        pmu_[0:1, o:o + w], ones_col_bf[:, :],
                        accbf[gb2][:, o:o + w], start=True, stop=True),
                        sem=None)
                    p2 = P.op("pe", lambda t, o=o, w=w, psq_=psq_: t.matmul(
                        psq_[0:1, o:o + w], ones_col_bf[:, :],
                        sqf[gb2][:, o:o + w], start=True, stop=True), sem="pe")
                    P.wait("dve", "pe", p2)
                    P.wait("dve", "dve", row2_dve.get("chain", 0))
                    if dc == 0:
                        P.op("dve", lambda v, o=o, w=w, pmu_=pmu_: v.tensor_copy(
                            muf_row[0:1, o:o + w], pmu_[0:1, o:o + w]),
                            sem="dve")
                        d2 = P.op("dve", lambda v, o=o, w=w, psq_=psq_:
                                  v.tensor_copy(
                                      mu2f_row[0:1, o:o + w],
                                      psq_[0:1, o:o + w]), sem="dve")
                    else:
                        P.op("dve", lambda v, o=o, w=w, pmu_=pmu_: v.tensor_add(
                            muf_row[0:1, o:o + w], muf_row[0:1, o:o + w],
                            pmu_[0:1, o:o + w]), sem="dve")
                        d2 = P.op("dve", lambda v, o=o, w=w, psq_=psq_:
                                  v.tensor_add(
                                      mu2f_row[0:1, o:o + w],
                                      mu2f_row[0:1, o:o + w],
                                      psq_[0:1, o:o + w]), sem="dve")
                    row2_dve[("r", gb2)] = d2
                    row2_dve["chain"] = d2
                st_pe[gb2] = P.cnt["pe"]
            pendf.append(emit_statsf)
            if len(pendf) > 1:
                pendf.pop(0)()
        while pendf:
            pendf.pop(0)()
        dr = P.cnt["dve"]
        P.wait("dve", "dve", dr)
        P.op("dve", lambda v: v.tensor_scalar_mul(muf_row[:, :], muf_row[:, :],
                                                  1.0 / D), sem="dve")
        q2 = P.op("dve", lambda v: v.tensor_scalar_mul(
            mu2f_row[:, :], mu2f_row[:, :], 1.0 / D), sem="dve")
        P.wait("dve", "dve", q2)
        q3 = P.op("dve", lambda v: v.tensor_mul(tmpf_row[:, :], muf_row[:, :],
                                                muf_row[:, :]), sem="dve")
        P.wait("dve", "dve", q3)
        q4 = P.op("dve", lambda v: v.tensor_sub(mu2f_row[:, :], mu2f_row[:, :],
                                                tmpf_row[:, :]), sem="dve")
        P.wait("act", "dve", q4)
        q5 = P.op("act", lambda s: s.activation(mu2f_row[:, :], mu2f_row[:, :],
                                                AF.Sqrt, bias=fconst[0:1, 3:4]),
                  sem="act")
        P.wait("dve", "act", q5)
        q6 = P.op("dve", lambda v: v.reciprocal(mu2f_row[:, :], mu2f_row[:, :]),
                  sem="dve")
        dve_fin = P.cnt["dve"]
        pg_free[0] = (P.cnt["act"], dve_fin)
        pg_free[1] = (P.cnt["act"], dve_fin)
        P.wait("pe", "dve", dve_fin)
        P.wait("pe", "act", P.cnt["act"])
        for o, w in tile512(TT):
            P.op("pe", lambda t, o=o, w=w: t.matmul(
                po[0][:, o:o + w], ones_row_f[:, :], muf_row[0:1, o:o + w],
                start=True, stop=True), sem=None)
            sem = "pe" if o + w == TT else None
            P.op("pe", lambda t, o=o, w=w: t.matmul(
                po[1][:, o:o + w], ones_row_f[:, :], mu2f_row[0:1, o:o + w],
                start=True, stop=True), sem=sem)
        pe_bcf = P.cnt["pe"]
        for dc in range(CD):
            osl = dc % 2
            P.wait("dve", "pe", pe_bcf)
            P.wait("dve", "os%d" % osl, of_war.get(osl, 0))
            d1 = P.op("dve", lambda v, dc=dc, osl=osl: v.tensor_sub(
                of_sb[osl][:, :], acc[:, dc * TT:(dc + 1) * TT], po[0][:, :]),
                sem="dve")
            P.wait("dve", "dve", d1)
            d2 = P.op("dve", lambda v, osl=osl: v.tensor_mul(
                of_sb[osl][:, :], of_sb[osl][:, :], po[1][:, :]), sem="dve")
            P.wait("sp", "dve", d2)
            so1 = P.op("sp", lambda e_, dc=dc, osl=osl, ts0=ts0: e_.dma_start(
                out_fm[dc * 128:(dc + 1) * 128, ts0:ts0 + TT], of_sb[osl][:, :]),
                sem="os%d" % osl, inc=16)
            of_war[osl] = so1
        if cfg.safe_barriers in ("both", "tt") and tt == 0:
            P.barrier()
        ln2_dve_end[0] = P.cnt["dve"]
        po_free[0] = P.cnt["dve"]
        po_free[1] = P.cnt["dve"]

    P.wait("sp", "so", P.cnt["so"])
    P.wait("sp", "os0", P.cnt["os0"])
    P.wait("sp", "os1", P.cnt["os1"])
    P.finish()
    return nc, P



def yarn_tables(S):
    DR = 64
    pos2 = np.arange(0, DR, 2, dtype=np.float64)
    inv = 1.0 / (10000.0 ** (pos2 / DR))
    wavelen = 2.0 * np.pi / inv
    ramp = np.clip((8192.0 / wavelen - 1.0) / 3.0, 0.0, 1.0)
    inv = inv * (1.0 - ramp) + inv / 16.0 * ramp
    t = np.arange(S, dtype=np.float64)
    freqs = np.outer(t, inv)
    emb = np.concatenate([freqs, freqs], axis=-1)
    af = 0.1 * math.log(16.0) + 1.0
    cos_t = (np.cos(emb) * af).T.astype(np.float32).copy()
    sin = (np.sin(emb) * af).T.astype(np.float32)
    sin_s = sin.copy()
    sin_s[:32] = -sin_s[:32]
    return cos_t, np.ascontiguousarray(sin_s)


def host_prepare(cfg, inp):
    S, D, CD, QC, KC, H, FC, E, SH, NE = (cfg.S, cfg.D, cfg.CD, cfg.QC, cfg.KC,
                                          cfg.H, cfg.FC, cfg.E, cfg.SH, cfg.NE)
    QL, KL, F = cfg.QL, cfg.KL, cfg.F
    F2, D2 = F // 2, D // 2
    f32 = np.float32
    o = {}
    wqa = np.asarray(inp["L0_attn_w_qa"], f32)
    o["wqa_re"] = np.ascontiguousarray(
        wqa.reshape(CD, 128, 4, QL // 4).transpose(2, 1, 0, 3).reshape(
            4, 128, CD * (QL // 4)))
    o["gq_re"] = np.ascontiguousarray(
        np.asarray(inp["L0_attn_q_norm_w"], f32).reshape(QC, 128).T)
    wqb = np.asarray(inp["L0_attn_w_qb"], f32).reshape(QL, H, 192)
    o["wqb_n_re"] = np.ascontiguousarray(
        wqb[:, :, :128].reshape(QC, 128, H, 128).transpose(2, 1, 0, 3).reshape(
            H, 128, QC * 128))
    r2 = wqb[:, :, 128:].reshape(QC, 128, H // 2, 2, 64).transpose(2, 1, 0, 3, 4)
    o["wqb_r2_re"] = np.ascontiguousarray(r2.reshape(H // 2, 128, QC * 128))
    wkva = np.asarray(inp["L0_attn_w_kva"], f32)
    o["wkva_re"] = np.ascontiguousarray(
        wkva.reshape(CD, 128, KL + 64).transpose(1, 0, 2).reshape(
            128, CD * (KL + 64)))
    o["gkv_re"] = np.ascontiguousarray(
        np.asarray(inp["L0_attn_kv_norm_w"], f32).reshape(KC, 128).T)
    wkvb = np.asarray(inp["L0_attn_w_kvb"], f32).reshape(KL, H, 256)
    o["wkvb_k_re"] = np.ascontiguousarray(
        wkvb[:, :, :128].reshape(KC, 128, H, 128).transpose(2, 1, 0, 3).reshape(
            H, 128, KC * 128))
    o["wkvb_v_re"] = np.ascontiguousarray(
        wkvb[:, :, 128:].reshape(KC, 128, H, 128).transpose(2, 1, 0, 3).reshape(
            H, 128, KC * 128))
    wo = np.asarray(inp["L0_attn_w_o"], f32)
    o["wo_re"] = np.ascontiguousarray(
        wo.reshape(H, 128, CD, 128).transpose(2, 1, 0, 3).reshape(
            CD, 128, H * 128))
    wa = np.asarray(inp["L0_ffn_wa"], f32)
    o["wa_rr"] = np.ascontiguousarray(
        wa.reshape(CD, 128, E).transpose(1, 0, 2).reshape(128, CD * E))
    rgb = np.asarray(inp["L0_ffn_rgb"], f32)
    sgb = np.asarray(inp["L0_ffn_sgb"], f32)
    gb = np.concatenate([rgb.reshape(E, FC, 128), sgb.reshape(SH, FC, 128)], 0)
    o["gb_re"] = np.ascontiguousarray(gb.transpose(2, 0, 1).reshape(128, NE * FC))
    rb1 = np.asarray(inp["L0_ffn_rb1"], f32)
    sb1 = np.asarray(inp["L0_ffn_sb1"], f32)
    b1 = np.concatenate([rb1.reshape(E, FC, 128), sb1.reshape(SH, FC, 128)], 0)
    o["b1_re"] = np.ascontiguousarray(b1.transpose(2, 0, 1).reshape(128, NE * FC))
    o["b2r_re"] = np.ascontiguousarray(np.asarray(inp["L0_ffn_rb2"], f32))
    sb2 = np.asarray(inp["L0_ffn_sb2"], f32)
    o["sb2_cols"] = np.ascontiguousarray(
        sb2.reshape(SH, CD, 128).transpose(2, 0, 1).reshape(128, SH * CD))

    def w_in(a, n):
        return np.ascontiguousarray(
            a.reshape(n, CD, 128, 2, F2).transpose(0, 3, 2, 1, 4).reshape(
                n, 2, 128, CD * F2))

    def w_out(a, n):
        return np.ascontiguousarray(
            a.reshape(n, FC, 128, 2, D2).transpose(0, 3, 2, 1, 4).reshape(
                n, 2, 128, FC * D2))

    o["rg_re"] = w_in(np.asarray(inp["L0_ffn_rg"], f32), E)
    o["rw1_re"] = w_in(np.asarray(inp["L0_ffn_rw1"], f32), E)
    o["rw2_re"] = w_out(np.asarray(inp["L0_ffn_rw2"], f32), E)
    o["sg_re"] = w_in(np.asarray(inp["L0_ffn_sg"], f32), SH)
    o["sw1_re"] = w_in(np.asarray(inp["L0_ffn_sw1"], f32), SH)
    o["sw2_re"] = w_out(np.asarray(inp["L0_ffn_sw2"], f32), SH)
    es = np.zeros((E, E * 128), f32)
    for e in range(E):
        es[e, e * 128:(e + 1) * 128] = 1.0
    o["esel_re"] = es
    cos_t, sin_s = yarn_tables(S)
    o["cos_re"] = cos_t
    o["sin_re"] = sin_s
    m = np.zeros((128, 128), np.float32)
    for i in range(128):
        m[i, :i] = NEG
    o["maskT_re"] = m
    o["ident_re"] = np.eye(128, dtype=f32)
    return o


_CACHE = {}


def kernel(**inputs):
    cfg = Cfg()
    if "full" not in _CACHE:
        _CACHE["full"] = build(cfg)
    nc, P = _CACHE["full"]
    shared = host_prepare(cfg, inputs)
    x = np.asarray(inputs["L0_attn_x"], np.float32)
    in_maps = []
    for c in range(8):
        m = dict(shared)
        m["x_fm"] = np.ascontiguousarray(x[c].T)
        in_maps.append(m)
    res = run_bass_kernel_spmd(nc, in_maps, core_ids=list(range(8)))
    return np.stack([np.asarray(r["out_fm"]).T for r in res.results], axis=0)
